# version 22
# baseline (speedup 1.0000x reference)
import numpy as np
from contextlib import ExitStack

import concourse.bass as bass
import concourse.mybir as mybir
from concourse.bass_utils import run_bass_kernel_spmd

F32 = mybir.dt.float32
F32R = mybir.dt.float32r
U8 = mybir.dt.uint8
AF = mybir.ActivationFunctionType
AL = mybir.AluOpType

D = 1024
DI = 2048
NSTATE = 16
H = 8
PH = DI // H
DTR = 64
TT = 512
EPS_LN = 1e-5
EPS_RMS = 1e-6


def _ap(handle, off, dims):
    return bass.AP(handle, off, dims)


def _sv(apx, off, dims_after_part):
    return bass.AP(apx.tensor, apx.offset + off,
                   [[apx.ap[0][0], 128]] + dims_after_part)


def host_prep(inputs):
    W_in = np.asarray(inputs["L0_attn_W_in"])
    wA = np.concatenate([W_in[2048:4096], W_in[4160:4288], W_in[4288:4416],
                         W_in[4096:4160]], axis=0).T
    wAs = np.zeros((128, 18944), np.float32)
    for k in range(8):
        blk = wA[k * 128:(k + 1) * 128]
        for mc in range(16):
            wAs[:, (mc * 8 + k) * 128:(mc * 8 + k + 1) * 128] = \
                blk[:, mc * 128:(mc + 1) * 128]
        wAs[:, 16384 + k * 128:16384 + (k + 1) * 128] = blk[:, 2048:2176]
        wAs[:, 17408 + k * 128:17408 + (k + 1) * 128] = blk[:, 2176:2304]
        wAs[:, 18432 + k * 64:18432 + (k + 1) * 64] = blk[:, 2304:2368]

    def pack_km(wT, nk, nm):
        out = np.zeros((128, nk * nm * 128), np.float32)
        for k in range(nk):
            for mc in range(nm):
                out[:, (mc * nk + k) * 128:(mc * nk + k + 1) * 128] = \
                    wT[k * 128:(k + 1) * 128, mc * 128:(mc + 1) * 128]
        return out

    def pack_mk(wT, nk, nm):
        out = np.zeros((128, nk * nm * 128), np.float32)
        for k in range(nk):
            for mc in range(nm):
                out[:, (k * nm + mc) * 128:(k * nm + mc + 1) * 128] = \
                    wT[k * 128:(k + 1) * 128, mc * 128:(mc + 1) * 128]
        return out

    wZs = pack_km(W_in[0:2048].T.copy(), 8, 16)
    wRs = pack_km(np.asarray(inputs["L0_attn_W_res"]).T.copy(), 8, 16)
    wOs = pack_km(np.asarray(inputs["L0_attn_W_out"]).T.copy(), 16, 8)
    w1s = pack_km(np.asarray(inputs["L0_ffn_w1"]).copy(), 8, 32)
    w2s = pack_mk(np.asarray(inputs["L0_ffn_w2"]).copy(), 32, 8)
    wdts = np.asarray(inputs["L0_attn_W_dt"]).T.copy().astype(np.float32)

    convw = np.asarray(inputs["L0_attn_conv_w"])
    convw_c = convw.reshape(16, 128, 4).transpose(1, 0, 2).reshape(128, 64).copy()
    convb_c = np.asarray(inputs["L0_attn_conv_b"]).reshape(16, 128).T.copy()
    dskip_c = np.asarray(inputs["L0_attn_Dskip"]).reshape(2048).reshape(16, 128).T.copy()
    normw_c = np.asarray(inputs["L0_attn_norm_w"]).reshape(16, 128).T.copy()
    b1_c = np.asarray(inputs["L0_ffn_b1"]).reshape(32, 128).T.copy()
    b2_c = np.asarray(inputs["L0_ffn_b2"]).reshape(8, 128).T.copy()
    dtbias_c = np.asarray(inputs["L0_attn_dt_bias"]).reshape(8, 1).copy()
    alog_c = np.asarray(inputs["L0_attn_A_log"]).reshape(128, 1).copy()

    ones = np.ones((128, 128), np.float32)
    ident = np.eye(128, dtype=np.float32)
    e8 = np.zeros((8, 128), np.float32)
    for hh in range(8):
        e8[hh, hh * 16:(hh + 1) * 16] = 1.0
    hsel = np.zeros((128, 128), np.float32)
    for row in range(128):
        hsel[row, (row // 16)::8] = 1.0
    maskq = np.zeros((128, 128), np.uint8)
    for qs in range(128):
        hs, ss, cs = qs % 8, (qs // 8) % 4, qs // 32
        for qt in range(128):
            ht, st, ct = qt % 8, (qt // 8) % 4, qt // 32
            if hs == ht and cs == ct and ss <= st:
                maskq[qs, qt] = 1
    maskx = np.zeros((128, 96), np.uint8)
    for p in range(128):
        for c in range(96):
            if p % 8 == c % 8:
                maskx[p, c] = 1

    common = dict(wAs=wAs, wZs=wZs, wRs=wRs, wOs=wOs, w1s=w1s, w2s=w2s,
                  wdts=wdts, convw=convw_c.astype(np.float32),
                  convb=convb_c.astype(np.float32),
                  dskip=dskip_c.astype(np.float32),
                  normw=normw_c.astype(np.float32),
                  b1c=b1_c.astype(np.float32), b2c=b2_c.astype(np.float32),
                  dtbias=dtbias_c.astype(np.float32),
                  alog=alog_c.astype(np.float32),
                  ones=ones, ident=ident, e8=e8, hsel=hsel, maskq=maskq,
                  maskx=maskx, maskqf=maskq.astype(np.float32),
                  maskxf=maskx.astype(np.float32))
    return common


class MK:
    def __init__(self, L=4096, stages=("s0", "s1a", "s1b", "s2", "s3z", "s3r",
                                       "s3ga", "s3gb", "s4", "s5"),
                 dev=False):
        self.nc = bass.Bass("TRN2", target_bir_lowering=False)
        self.L = L
        self.NTT = L // TT
        self.NP = L // 16
        self.NG = L // 128
        self.FB = min(512, L)
        self.NFB = L // self.FB
        self.stages = set(stages)
        self.dev = dev
        self.es = ExitStack()
        self.sems = {}
        self.cnt = {}
        self.chain = {}
        self.bind = {}
        self.pcnt = {}
        self.freelist = {"hwd": [], "swd": [], "eng": []}
        self._declare_dram()

    GLOBAL_SEMS = ("vch", "sch", "gch", "ldc")

    def _sem_class(self, name):
        if name.startswith("ld"):
            return "hwd"
        if name.startswith("st"):
            return "swd"
        return "eng"

    def sem(self, name):
        if name not in self.bind:
            cls = self._sem_class(name)
            if self.freelist[cls]:
                h = self.freelist[cls].pop()
            else:
                h = self.es.enter_context(self.nc.semaphore())
                self.pcnt[h] = 0
            self.bind[name] = (h, self.pcnt[h])
            self.cnt[name] = 0
        return self.bind[name][0]

    def inc(self, bi, name, n=1):
        h = self.sem(name)
        bi.then_inc(h, n)
        self.pcnt[h] += n
        self.cnt[name] += n
        return self.cnt[name]

    def W(self, eng, name, v):
        h = self.sem(name)
        eng.wait_ge(h, self.bind[name][1] + v)

    def end_stage(self):
        self.chain = {k: v for k, v in self.chain.items()
                      if v[0] in self.GLOBAL_SEMS}
        self.nc.all_engine_barrier()
        for name in list(self.bind):
            if name in self.GLOBAL_SEMS:
                continue
            h, _ = self.bind.pop(name)
            self.freelist[self._sem_class(name)].append(h)

    def EV(self, eng, key, fn, *args, inc=None, n=1, **kw):
        prev = self.chain.get(key)
        if prev is not None and prev[0] in self.bind:
            self.W(eng, prev[0], prev[1])
        instr = getattr(eng, fn)(*args, **kw)
        nm = inc if inc is not None else (key + "ch")
        c = self.inc(instr, nm, n)
        self.chain[key] = (nm, c)
        return instr

    def _declare_dram(self):
        nc, L = self.nc, self.L
        def din(name, shape, dt=F32):
            return nc.dram_tensor(name, shape, dt, kind="ExternalInput")
        self.xT = din("xT", [D, L])
        self.wAs = din("wAs", [128, 18944])
        self.wZs = din("wZs", [128, 16384])
        self.wRs = din("wRs", [128, 16384])
        self.wOs = din("wOs", [128, 16384])
        self.w1s = din("w1s", [128, 32768])
        self.w2s = din("w2s", [128, 32768])
        self.wdts = din("wdts", [64, 8])
        self.convw = din("convw", [128, 64])
        self.convb = din("convb", [128, 16])
        self.dskip = din("dskip", [128, 16])
        self.normw = din("normw", [128, 16])
        self.b1c = din("b1c", [128, 32])
        self.b2c = din("b2c", [128, 8])
        self.dtbias = din("dtbias", [8, 1])
        self.alog = din("alog", [128, 1])
        self.ones = din("ones", [128, 128])
        self.ident = din("ident", [128, 128])
        self.e8 = din("e8", [8, 128])
        self.hsel = din("hsel", [128, 128])
        self.maskq = din("maskq", [128, 128], U8)
        self.maskx = din("maskx", [128, 96], U8)
        self.maskqf = din("maskqf", [128, 128])
        self.maskxf = din("maskxf", [128, 96])
        self.outT = nc.dram_tensor("outT", [D, L], F32, kind="ExternalOutput")

        kind = "ExternalOutput" if self.dev else "Internal"
        def scr(name, shape):
            return self.nc.dram_tensor(name, shape, F32, kind=kind)
        self.ln1h_d = scr("ln1h_d", [D, L])
        self.ln1l_d = scr("ln1l_d", [D, L])
        self.u_d = scr("u_d", [DI, L])
        self.U_d = scr("U_d", [L * 8, PH])
        self.y_d = scr("y_d", [L * 8, PH])
        self.sz_d = scr("sz_d", [DI, L])
        self.r_d = scr("r_d", [DI, L])
        self.g_d = scr("g_d", [DI, L])
        self.x2_d = scr("x2_d", [D, L])
        self.ln2_d = scr("ln2_d", [D, L])
        self.ln2h_d = scr("ln2h_d", [D, L])
        self.ln2l_d = scr("ln2l_d", [D, L])
        self.hgh_d = scr("hgh_d", [4096, L])
        self.hgl_d = scr("hgl_d", [4096, L])
        self.Bc_d = scr("Bc_d", [128, L])
        self.Cc_d = scr("Cc_d", [128, L])
        self.dtr_d = scr("dtr_d", [128, L])
        if self.dev:
            self.dbgB = scr("dbgB", [128, L])
            self.dbgC = scr("dbgC", [128, L])
            self.dbgdt = scr("dbgdt", [8, L])

    def _alloc_persistent(self):
        nc = self.nc
        a = nc.alloc_sbuf_tensor
        self.ones_s = a("ones_s", [128, 128], F32)
        self.ident_s = a("ident_s", [128, 128], F32)
        self.convw_s = a("convw_s", [128, 64], F32)
        self.convb_s = a("convb_s", [128, 16], F32)
        self.dskip_s = a("dskip_s", [128, 16], F32)
        self.normw_s = a("normw_s", [128, 16], F32)
        self.b1_s = a("b1_s", [128, 32], F32)
        self.b2_s = a("b2_s", [128, 8], F32)
        self.dtbias_s = a("dtbias_s", [8, 1], F32)
        self.alog_s = a("alog_s", [128, 1], F32)
        self.A_s = a("A_s", [128, 1], F32)
        self.Aexp_s = a("Aexp_s", [128, 1], F32)
        self.e8_s = a("e8_s", [8, 128], F32)
        self.hsel_s = a("hsel_s", [128, 128], F32)
        self.maskq_s = a("maskq_s", [128, 128], U8)
        self.maskx_s = a("maskx_s", [128, 96], U8)
        self.zero128 = a("zero128", [128, 128], F32)
        self.c_one = a("c_one", [128, 1], F32)
        self.c_epsln = a("c_epsln", [128, 1], F32)
        self.c_epsrms = a("c_epsrms", [128, 1], F32)
        self.wdt_s = a("wdt_s", [64, 8], F32)


    def s0(self):
        nc = self.nc
        self._alloc_persistent()
        with nc.Block("s0") as blk:
            @blk.sync
            def _(sync):
                for dst, srcd in [(self.ones_s, self.ones), (self.ident_s, self.ident),
                                  (self.convw_s, self.convw), (self.convb_s, self.convb),
                                  (self.dskip_s, self.dskip), (self.normw_s, self.normw),
                                  (self.b1_s, self.b1c), (self.b2_s, self.b2c),
                                  (self.dtbias_s, self.dtbias), (self.alog_s, self.alog),
                                  (self.e8_s, self.e8), (self.hsel_s, self.hsel),
                                  (self.maskq_s, self.maskq), (self.maskx_s, self.maskx),
                                  (self.wdt_s, self.wdts)]:
                    self.inc(sync.dma_start(out=dst.ap(), in_=srcd.ap()), "ldc", 16)

            n_ldc = self.cnt["ldc"]

            @blk.vector
            def _(vector):
                self.EV(vector, "v", "memset", self.zero128.ap(), 0.0)
                self.EV(vector, "v", "memset", self.c_one.ap(), 1.0)
                self.EV(vector, "v", "memset", self.c_epsln.ap(), EPS_LN)
                self.EV(vector, "v", "memset", self.c_epsrms.ap(), EPS_RMS)
                self.W(vector, "seA", 1)
                self.EV(vector, "v", "tensor_scalar_mul",
                        self.A_s.ap(), self.Aexp_s.ap(), -1.0, inc="vA")

            @blk.scalar
            def _(scalar):
                self.W(scalar, "ldc", n_ldc)
                self.EV(scalar, "s", "activation", out=self.Aexp_s.ap(),
                        in_=self.alog_s.ap(), func=AF.Exp, inc="seA")
        self.end_stage()

    def s1a(self):
        nc, L, NTT = self.nc, self.L, self.NTT
        n_ldc = self.cnt["ldc"]
        with ExitStack() as st:
            def sb(name, shape, dt=F32):
                return st.enter_context(nc.sbuf_tensor(name, shape, dt)).ap()
            wu = sb("wu", [128, 16384])
            xin = [sb(f"xin{j}", [128, 4096]) for j in range(2)]
            xsq = [sb(f"xsq{j}", [128, TT]) for j in range(2)]
            ln1b = sb("ln1b", [128, 4096])
            ln1h = sb("ln1h", [128, 4096], F32R)
            ln1l = sb("ln1l", [128, 4096], F32R)
            ub = [sb(f"ub{j}", [128, TT]) for j in range(4)]
            u_tm = [sb(f"utm{j}", [128, 2048]) for j in range(4)]
            halo = sb("halo", [128, 48])
            mu = sb("mu", [128, TT]); musq = sb("musq", [128, TT])
            vbuf = sb("vbuf", [128, TT]); vtmp = sb("vtmp", [128, TT])
            rstd = sb("rstd", [128, TT]); murstd = sb("murstd", [128, TT])
            tmp5 = sb("tmp5", [128, TT])
            pSA = st.enter_context(nc.psum_tensor("pSA", [128, 512], F32)).ap()
            pSB = st.enter_context(nc.psum_tensor("pSB", [128, 512], F32)).ap()
            pP = [st.enter_context(nc.psum_tensor(f"pP{j}", [128, 512], F32)).ap()
                  for j in range(2)]
            pT = [st.enter_context(nc.psum_tensor(f"pT{j}", [128, 512], F32)).ap()
                  for j in range(2)]
            pE = st.enter_context(nc.psum_tensor("pE1", [128, 512], F32)).ap()

            with nc.Block("s1a") as blk:
                @blk.sync
                def _(sync):
                    self.inc(sync.dma_start(out=wu, in_=self.wAs.ap()[:, 0:16384]),
                             "ldw1a", 16)
                    for i in range(NTT):
                        if i >= 2:
                            self.W(sync, "xfree", i - 1)
                            self.W(sync, "teA", i - 1)
                            self.W(sync, "sq", 8 * (i - 1))
                        srcx = _ap(self.xT, i * TT, [[L, 128], [128 * L, 8], [1, TT]])
                        self.inc(sync.dma_start(out=xin[i % 2], in_=srcx),
                                 f"ldx{i % 2}", 16)

                @blk.vector
                def _(vector):
                    self.EV(vector, "v", "memset", halo, 0.0)
                    for i in range(NTT):
                        self.W(vector, "teA", i + 1)
                        self.EV(vector, "v", "tensor_scalar_mul", mu, pSA[:, 0:TT],
                                1.0 / D)
                        self.EV(vector, "v", "tensor_mul", musq, mu, mu)
                        self.W(vector, "teSq", 8 * (i + 1))
                        if i > 0:
                            self.W(vector, "rstdS", i)
                        self.EV(vector, "v", "scalar_tensor_tensor", inc="vstat",
                                out=vbuf, in0=pSB[:, 0:TT], scalar=1.0 / D,
                                in1=musq, op0=AL.mult, op1=AL.subtract)
                        self.W(vector, "rstdS", i + 1)
                        if i > 0:
                            self.W(vector, "teP", i * 16)
                        self.EV(vector, "v", "tensor_mul", murstd, mu, rstd)
                        for k in range(8):
                            ksl = xin[i % 2][:, k * TT:(k + 1) * TT]
                            self.EV(vector, "v", "tensor_mul", tmp5, ksl, rstd,
                                    inc=("xfree" if k == 7 else None))
                            self.EV(vector, "v", "tensor_sub",
                                    ln1b[:, k * TT:(k + 1) * TT], tmp5, murstd,
                                    inc="ln1r")
                        if i > 0:
                            self.W(vector, "stLn", 32 * i)
                        for k in range(8):
                            bsl = ln1b[:, k * TT:(k + 1) * TT]
                            self.EV(vector, "v", "scalar_tensor_tensor",
                                    out=pE[:, 0:TT], in0=bsl, scalar=4097.0,
                                    in1=bsl, op0=AL.mult, op1=AL.subtract)
                            self.EV(vector, "v", "scalar_tensor_tensor",
                                    out=ln1h[:, k * TT:(k + 1) * TT], in0=bsl,
                                    scalar=4097.0, in1=pE[:, 0:TT],
                                    op0=AL.mult, op1=AL.subtract)
                            self.EV(vector, "v", "tensor_sub",
                                    ln1l[:, k * TT:(k + 1) * TT], bsl,
                                    ln1h[:, k * TT:(k + 1) * TT],
                                    inc=("ln1s" if k == 7 else None))
                        for fc in range(16):
                            self.W(vector, "teP", i * 16 + fc + 1)
                            j = i * 16 + fc
                            if j >= 4:
                                self.W(vector, "teT", (j - 4) * 4 + 4)
                                self.W(vector, f"stU{fc % 4}", 16 * (i * 4 + fc // 4))
                            ps = pP[fc % 2]
                            dst = ub[fc % 4]
                            wcol = lambda k: self.convw_s.ap()[:, fc * 4 + k:fc * 4 + k + 1]
                            bcol = self.convb_s.ap()[:, fc:fc + 1]
                            self.EV(vector, "v", "tensor_scalar", out=dst,
                                    in0=ps[:, 0:512], scalar1=wcol(3), scalar2=bcol,
                                    op0=AL.mult, op1=AL.add)
                            for k in range(3):
                                off = 3 - k
                                self.EV(vector, "v", "scalar_tensor_tensor",
                                        out=dst[:, off:512], in0=ps[:, 0:512 - off],
                                        scalar=wcol(k), in1=dst[:, off:512],
                                        op0=AL.mult, op1=AL.add)
                                self.EV(vector, "v", "scalar_tensor_tensor",
                                        out=dst[:, 0:off],
                                        in0=halo[:, fc * 3 + k:fc * 3 + 3],
                                        scalar=wcol(k), in1=dst[:, 0:off],
                                        op0=AL.mult, op1=AL.add)
                            self.EV(vector, "v", "tensor_copy",
                                    out=halo[:, fc * 3:fc * 3 + 3],
                                    in_=ps[:, 509:512], inc="cv")

                @blk.scalar
                def _(scalar):
                    def squares(i):
                        self.W(scalar, f"ldx{i % 2}", 16 * (i // 2 + 1))
                        for k in range(8):
                            if i * 8 + k >= 2:
                                self.W(scalar, "teSq", i * 8 + k - 1)
                            self.EV(scalar, "s", "activation", out=xsq[k % 2],
                                    in_=xin[i % 2][:, k * TT:(k + 1) * TT],
                                    func=AF.Square, inc="sq")
                    def rstds(i):
                        self.W(scalar, "vstat", i + 1)
                        if i > 0:
                            self.W(scalar, "xfree", i)
                        self.EV(scalar, "s", "activation", out=vtmp, in_=vbuf,
                                func=AF.Ln, bias=self.c_epsln.ap())
                        self.EV(scalar, "s", "activation", out=rstd, in_=vtmp,
                                func=AF.Exp, scale=-0.5, inc="rstdS")
                    squares(0)
                    rstds(0)
                    for i in range(NTT):
                        if i + 1 < NTT:
                            squares(i + 1)
                        for fc in range(16):
                            for g in range(4):
                                idx = i * 64 + fc * 4 + g
                                self.W(scalar, "teT", idx + 1)
                                if i > 0 and fc == 0 and g == 0:
                                    self.W(scalar, "stUM", 16 * i * 4)
                                self.EV(scalar, "s", "copy",
                                        out=u_tm[g][:, fc * 128:(fc + 1) * 128],
                                        in_=pT[idx % 2][:, 0:128], inc="evT")
                        if i + 1 < NTT:
                            rstds(i + 1)

                @blk.tensor
                def _(tensor):
                    self.W(tensor, "ldw1a", 16)
                    self.W(tensor, "ldc", n_ldc)
                    def stats(i):
                        self.W(tensor, f"ldx{i % 2}", 16 * (i // 2 + 1))
                        if i > 0:
                            self.W(tensor, "vstat", i)
                        for k in range(8):
                            it = tensor.matmul(pSA[:, 0:TT], self.ones_s.ap(),
                                               xin[i % 2][:, k * TT:(k + 1) * TT],
                                               start=(k == 0), stop=(k == 7))
                        self.inc(it, "teA", 1)
                        for k in range(8):
                            self.W(tensor, "sq", i * 8 + k + 1)
                            it = tensor.matmul(pSB[:, 0:TT], self.ones_s.ap(),
                                               xsq[k % 2], start=(k == 0),
                                               stop=(k == 7))
                            self.inc(it, "teSq", 1)
                    def body(i):
                        self.W(tensor, "ln1r", 8 * (i + 1))
                        def transposes(fc):
                            self.W(tensor, "cv", i * 16 + fc + 1)
                            for g in range(4):
                                idx = i * 64 + fc * 4 + g
                                if idx >= 2:
                                    self.W(tensor, "evT", idx - 1)
                                it2 = tensor.transpose(pT[idx % 2][:, 0:128],
                                                       ub[fc % 4][:, g * 128:(g + 1) * 128],
                                                       self.ident_s.ap())
                                self.inc(it2, "teT", 1)
                        for mc in range(16):
                            if i * 16 + mc >= 2:
                                self.W(tensor, "cv", i * 16 + mc - 1)
                            for k in range(8):
                                it = tensor.matmul(
                                    pP[mc % 2][:, 0:TT],
                                    wu[:, (mc * 8 + k) * 128:(mc * 8 + k + 1) * 128],
                                    ln1b[:, k * TT:(k + 1) * TT],
                                    start=(k == 0), stop=(k == 7))
                            self.inc(it, "teP", 1)
                            if mc >= 2:
                                transposes(mc - 2)
                        transposes(14)
                        transposes(15)
                    stats(0)
                    for i in range(NTT):
                        if i + 1 < NTT:
                            stats(i + 1)
                        body(i)

                @blk.gpsimd
                def _(gpsimd):
                    for i in range(NTT):
                        self.W(gpsimd, "ln1s", i + 1)
                        dst = _ap(self.ln1h_d, i * TT,
                                  [[L, 128], [128 * L, 8], [1, TT]]).bitcast(F32R)
                        self.inc(gpsimd.dma_start(out=dst, in_=ln1h), "stLn", 16)
                        dst = _ap(self.ln1l_d, i * TT,
                                  [[L, 128], [128 * L, 8], [1, TT]]).bitcast(F32R)
                        self.inc(gpsimd.dma_start(out=dst, in_=ln1l), "stLn", 16)
                        for fc in range(16):
                            self.W(gpsimd, "cv", i * 16 + fc + 1)
                            dst = _ap(self.u_d, fc * 128 * L + i * TT,
                                      [[L, 128], [1, TT]])
                            self.inc(gpsimd.dma_start(out=dst, in_=ub[fc % 4]),
                                     f"stU{fc % 4}", 16)
                        for g in range(4):
                            self.W(gpsimd, "evT", i * 64 + 15 * 4 + g + 1)
                            tok0 = i * TT + g * 128
                            dst = _ap(self.U_d, tok0 * 8 * PH, [[2048, 128], [1, 2048]])
                            self.inc(gpsimd.dma_start(out=dst, in_=u_tm[g]), "stUM", 16)
                    self.W(gpsimd, "stUM", self.cnt["stUM"])
                    for s4 in range(4):
                        if self.cnt.get(f"stU{s4}", 0):
                            self.W(gpsimd, f"stU{s4}", self.cnt[f"stU{s4}"])
                    self.W(gpsimd, "stLn", self.cnt["stLn"])
        self.end_stage()

    def s1b(self):
        nc, L, NTT = self.nc, self.L, self.NTT
        with ExitStack() as st:
            def sb(name, shape, dt=F32):
                return st.enter_context(nc.sbuf_tensor(name, shape, dt)).ap()
            wbc = sb("wbc", [128, 2560])
            wbce = sb("wbce", [128, 2560])
            wbch = sb("wbch", [128, 2560], F32R)
            wbcl = sb("wbcl", [128, 2560], F32R)
            linh = [sb(f"linh{j}", [128, 4096], F32R) for j in range(2)]
            linl = [sb(f"linl{j}", [128, 4096], F32R) for j in range(2)]
            dthb = sb("dthb", [64, TT])
            tmp8 = sb("tmp8", [8, TT])
            dtb = sb("dtb1", [8, TT])
            Bbuf = [sb(f"Bbuf{j}", [128, TT]) for j in range(2)]
            Cbuf = [sb(f"Cbuf{j}", [128, TT]) for j in range(2)]
            drb = [sb(f"drb{j}", [128, TT]) for j in range(2)]
            pB = st.enter_context(nc.psum_tensor("pB1", [128, 512], F32)).ap()
            pC = st.enter_context(nc.psum_tensor("pC1", [128, 512], F32)).ap()
            pDH = st.enter_context(nc.psum_tensor("pDH1", [128, 512], F32)).ap()
            pDT = st.enter_context(nc.psum_tensor("pDT1", [128, 512], F32)).ap()
            pRE = st.enter_context(nc.psum_tensor("pRE1", [128, 512], F32)).ap()

            with nc.Block("s1b") as blk:
                @blk.sync
                def _(sync):
                    self.inc(sync.dma_start(out=wbc, in_=self.wAs.ap()[:, 16384:18944]),
                             "ldw1b", 16)
                    for i in range(NTT):
                        if i >= 2:
                            self.W(sync, "teDH1", i - 1)
                        srcl = _ap(self.ln1h_d, i * TT,
                                   [[L, 128], [128 * L, 8], [1, TT]]).bitcast(F32R)
                        self.inc(sync.dma_start(out=linh[i % 2], in_=srcl),
                                 f"ldlh{i % 2}", 16)
                        srcl = _ap(self.ln1l_d, i * TT,
                                   [[L, 128], [128 * L, 8], [1, TT]]).bitcast(F32R)
                        self.inc(sync.dma_start(out=linl[i % 2], in_=srcl),
                                 f"ldll{i % 2}", 16)

                @blk.vector
                def _(vector):
                    self.W(vector, "ldw1b", 16)
                    self.EV(vector, "v", "scalar_tensor_tensor", out=wbce,
                            in0=wbc, scalar=4097.0, in1=wbc,
                            op0=AL.mult, op1=AL.subtract)
                    self.EV(vector, "v", "scalar_tensor_tensor", out=wbch,
                            in0=wbc, scalar=4097.0, in1=wbce,
                            op0=AL.mult, op1=AL.subtract)
                    self.EV(vector, "v", "tensor_sub", wbcl, wbc, wbch,
                            inc="wsp1b")
                    for i in range(NTT):
                        self.W(vector, "spS", i + 1)
                        if i >= 1:
                            self.W(vector, "teRE", i)
                        self.EV(vector, "v", "tensor_scalar",
                                out=dtb, in0=tmp8,
                                scalar1=1e-4, scalar2=1.0, op0=AL.max, op1=AL.min,
                                inc="dtr")

                @blk.tensor
                def _(tensor):
                    self.W(tensor, "wsp1b", 1)
                    def mm3(psum, coff, cw, i, name):
                        for k in range(8):
                            it = tensor.matmul(
                                psum, _ap(wbch.tensor, coff + k * cw,
                                          [[2560, 128], [1, cw]]),
                                linh[i % 2][:, k * TT:(k + 1) * TT],
                                start=(k == 0), stop=False)
                            it = tensor.matmul(
                                psum, _ap(wbch.tensor, coff + k * cw,
                                          [[2560, 128], [1, cw]]),
                                linl[i % 2][:, k * TT:(k + 1) * TT],
                                start=False, stop=False)
                            it = tensor.matmul(
                                psum, _ap(wbcl.tensor, coff + k * cw,
                                          [[2560, 128], [1, cw]]),
                                linh[i % 2][:, k * TT:(k + 1) * TT],
                                start=False, stop=(k == 7))
                        self.inc(it, name, 1)
                    for i in range(NTT):
                        self.W(tensor, f"ldlh{i % 2}", 16 * (i // 2 + 1))
                        self.W(tensor, f"ldll{i % 2}", 16 * (i // 2 + 1))
                        if i >= 1:
                            self.W(tensor, "evB", i)
                            self.W(tensor, "evC", i)
                            self.W(tensor, "evDTH", i)
                        mm3(pB[:, 0:TT], 0, 128, i, "teB1")
                        mm3(pC[:, 0:TT], 1024, 128, i, "teC1")
                        mm3(pDH[0:64, 0:TT], 2048, 64, i, "teDH1")
                        self.W(tensor, "evDTH", i + 1)
                        if i >= 1:
                            self.W(tensor, "spS", i)
                        it = tensor.matmul(pDT[0:8, 0:TT], self.wdt_s.ap(), dthb,
                                           start=True, stop=True)
                        self.inc(it, "teDT", 1)
                        self.W(tensor, "dtr", i + 1)
                        if i >= 1:
                            self.W(tensor, "evRE", i)
                        it = tensor.matmul(pRE[:, 0:TT], self.e8_s.ap(), dtb,
                                           start=True, stop=True)
                        self.inc(it, "teRE", 1)

                @blk.scalar
                def _(scalar):
                    for i in range(NTT):
                        self.W(scalar, "teB1", i + 1)
                        if i >= 2:
                            self.W(scalar, f"stB{i % 2}", 16 * (i // 2))
                        self.EV(scalar, "s", "copy", out=Bbuf[i % 2],
                                in_=pB[:, 0:TT], inc="evB")
                        self.W(scalar, "teC1", i + 1)
                        if i >= 2:
                            self.W(scalar, f"stC{i % 2}", 16 * (i // 2))
                        self.EV(scalar, "s", "copy", out=Cbuf[i % 2],
                                in_=pC[:, 0:TT], inc="evC")
                        self.W(scalar, "teDH1", i + 1)
                        if i >= 1:
                            self.W(scalar, "teDT", i)
                        self.EV(scalar, "s", "copy", out=dthb, in_=pDH[0:64, 0:TT],
                                inc="evDTH")
                        self.W(scalar, "teDT", i + 1)
                        if i >= 1:
                            self.W(scalar, "dtr", i)
                        self.EV(scalar, "s", "activation", out=tmp8,
                                in_=pDT[0:8, 0:TT], func=AF.Exp,
                                bias=self.dtbias_s.ap())
                        self.EV(scalar, "s", "activation", out=tmp8, in_=tmp8,
                                func=AF.Ln, bias=self.c_one.ap()[0:8, :], inc="spS")
                        self.W(scalar, "teRE", i + 1)
                        if i >= 2:
                            self.W(scalar, f"stDR{i % 2}", 16 * (i // 2))
                        self.EV(scalar, "s", "copy", out=drb[i % 2],
                                in_=pRE[:, 0:TT], inc="evRE")

                @blk.gpsimd
                def _(gpsimd):
                    for i in range(NTT):
                        self.W(gpsimd, "evB", i + 1)
                        self.inc(gpsimd.dma_start(
                            out=self.Bc_d.ap()[:, i * TT:(i + 1) * TT],
                            in_=Bbuf[i % 2]), f"stB{i % 2}", 16)
                        self.W(gpsimd, "evC", i + 1)
                        self.inc(gpsimd.dma_start(
                            out=self.Cc_d.ap()[:, i * TT:(i + 1) * TT],
                            in_=Cbuf[i % 2]), f"stC{i % 2}", 16)
                        self.W(gpsimd, "evRE", i + 1)
                        self.inc(gpsimd.dma_start(
                            out=self.dtr_d.ap()[:, i * TT:(i + 1) * TT],
                            in_=drb[i % 2]), f"stDR{i % 2}", 16)
                    for j in range(2):
                        for nm in (f"stB{j}", f"stC{j}", f"stDR{j}"):
                            if self.cnt.get(nm, 0):
                                self.W(gpsimd, nm, self.cnt[nm])
        self.end_stage()

    def s2(self):
        nc, L, NTT = self.nc, self.L, self.NTT
        NP = self.NP
        FB = self.FB
        NFB = self.NFB
        NPB = FB // 16
        NHG = NP // 8
        NHGF = NPB // 8
        with ExitStack() as st:
            def sb(name, shape, dt=F32):
                return st.enter_context(nc.sbuf_tensor(name, shape, dt)).ap()
            dt_rep = sb("dt_rep", [128, L])
            mkqf = sb("mkqf", [128, 128])
            mkxf = sb("mkxf", [128, 96])
            Bb = sb("Bb2", [128, L])
            Cb = sb("Cb2", [128, L])
            SCR = sb("SCR", [128, L])
            sB = sb("sBc", [128, L])
            nslice = [0]
            def argbuf(name):
                i = nslice[0]
                nslice[0] += 1
                if (i + 1) * FB <= L:
                    return SCR[:, i * FB:(i + 1) * FB]
                return sb(name, [128, FB])
            aAs = argbuf("aAs"); aASr = argbuf("aASr"); aC = argbuf("aC")
            aD = argbuf("aD")
            axc = [argbuf(f"axc{j}") for j in range(3)]
            axb = [argbuf(f"axb{j}") for j in range(3)]
            tB1 = sb("tB1", [128, FB]); tB2 = sb("tB2", [128, FB])
            tB3 = sb("tB3", [128, FB])
            Cd = [sb(f"Cd{j}", [128, FB]) for j in range(2)]
            CX = [[sb(f"CXf{j}_{k}", [128, FB]) for k in range(2)] for j in range(3)]
            Bd = sb("Bdf", [128, FB]); BD = sb("BDf", [128, FB])
            CS = sb("CSf", [128, FB])
            BX = [sb(f"BXf{j}", [128, FB]) for j in range(3)]
            se = [sb(f"se{j}", [128, FB]) for j in range(2)]
            lam = [sb(f"lam{j}", [128, NPB]) for j in range(2)]
            BDIAG = [sb(f"BDIAG{j}", [128, 1024]) for j in range(2)]
            BDbd = [sb(f"BDbd{j}", [128, 1024]) for j in range(2)]
            CSbd = [sb(f"CSbd{j}", [128, 1024]) for j in range(2)]
            BXbd = [[sb(f"BXbd{j}_{k}", [128, 256]) for k in range(2)]
                    for j in range(3)]
            U_s = [sb(f"Us{j}", [128, 256]) for j in range(4)]
            Mm = [sb(f"Mm{j}", [128, 128]) for j in range(2)]
            BdT = [sb(f"BdT{j}", [128, 128]) for j in range(2)]
            ygr = [sb(f"ygr{j}", [128, 1024]) for j in range(2)]
            h_s = [sb(f"hs{j}", [128, 256]) for j in range(2)]
            pM = [st.enter_context(nc.psum_tensor(f"pM{j}", [128, 512], F32)).ap()
                  for j in range(2)]
            pX = st.enter_context(nc.psum_tensor("pX", [128, 512], F32)).ap()
            pTt = st.enter_context(nc.psum_tensor("pTt0", [128, 512], F32)).ap()
            pY = [st.enter_context(nc.psum_tensor(f"pY{j}", [128, 512], F32)).ap()
                  for j in range(2)]
            pDH2 = [st.enter_context(nc.psum_tensor(f"pDH2{j}", [128, 512], F32)).ap()
                    for j in range(2)]

            A = self.A_s.ap()
            NPAIR = L // 16

            with nc.Block("s2") as blk:
                @blk.vector
                def _(vector):
                    self.EV(vector, "v", "memset", h_s[0], 0.0, inc="hrdy")
                    self.W(vector, "ldbc", 16 * 5)
                    def cstep(dst, srcb, d):
                        self.EV(vector, "v", "tensor_tensor",
                                out=_ap(dst.tensor, dst.offset + d,
                                        [[L, 128], [16, NPAIR], [1, 16 - d]]),
                                in0=_ap(srcb.tensor, srcb.offset + d,
                                        [[L, 128], [16, NPAIR], [1, 16 - d]]),
                                in1=_ap(srcb.tensor, srcb.offset,
                                        [[L, 128], [16, NPAIR], [1, 16 - d]]),
                                op=AL.add)
                        self.EV(vector, "v", "tensor_copy",
                                out=_ap(dst.tensor, dst.offset,
                                        [[L, 128], [16, NPAIR], [1, d]]),
                                in_=_ap(srcb.tensor, srcb.offset,
                                        [[L, 128], [16, NPAIR], [1, d]]))
                    cstep(SCR, dt_rep, 1)
                    cstep(sB, SCR, 2)
                    cstep(SCR, sB, 4)
                    cstep(sB, SCR, 8)

                    for fb in range(NFB):
                        base = fb * FB
                        scum = sB[:, base:base + FB]
                        dtr_sl = dt_rep[:, base:base + FB]
                        Bsl = Bb[:, base:base + FB]
                        Csl = Cb[:, base:base + FB]
                        if fb >= 1:
                            self.W(vector, "sxp", 11 * fb)
                            self.W(vector, "gbd", 2 * NHGF * fb)
                        if fb >= 2:
                            self.W(vector, "teM", NPB * (fb - 1))
                            self.W(vector, "teMx", NPB * (fb - 1))
                        self.EV(vector, "v", "tensor_scalar_mul", aAs, scum, A,
                                inc="vAs")
                        self.EV(vector, "v", "tensor_copy", out=tB1,
                                in_=_ap(sB.tensor, base + 15,
                                        [[L, 128], [16, NPB], [0, 16]]))
                        self.EV(vector, "v", "tensor_scalar_mul", aASr, tB1, A,
                                inc="vASr")
                        self.EV(vector, "v", "tensor_copy", out=tB1,
                                in_=_ap(sB.tensor, base + 3,
                                        [[L, 128], [16, NPB], [4, 4], [0, 4]]))
                        self.EV(vector, "v", "memset",
                                _ap(tB2.tensor, tB2.offset,
                                    [[FB, 128], [16, NPB], [1, 4]]), 0.0)
                        self.EV(vector, "v", "tensor_copy",
                                out=_ap(tB2.tensor, tB2.offset + 4,
                                        [[FB, 128], [16, NPB], [4, 3], [1, 4]]),
                                in_=_ap(sB.tensor, base + 3,
                                        [[L, 128], [16, NPB], [4, 3], [0, 4]]))
                        self.EV(vector, "v", "tensor_add", tB3, tB1, tB2)
                        self.EV(vector, "v", "tensor_scalar", out=tB1, in0=tB3,
                                scalar1=A, scalar2=0.5, op0=AL.mult, op1=AL.mult)
                        self.EV(vector, "v", "tensor_sub", aC, aAs, tB1,
                                inc="vargC")
                        for j in range(3):
                            self.EV(vector, "v", "tensor_copy", out=tB1,
                                    in_=_ap(sB.tensor, base + j * 4 + 3,
                                            [[L, 128], [16, NPB], [0, 16]]))
                            self.EV(vector, "v", "tensor_scalar_mul", tB2, tB1, A)
                            self.EV(vector, "v", "tensor_sub", tB3, aAs, tB2)
                            self.EV(vector, "v", "tensor_scalar_min", axc[j],
                                    tB3, 0.0, inc=f"vxc{j}")
                            self.EV(vector, "v", "tensor_scalar", out=axb[j],
                                    in0=tB3, scalar1=-1.0, scalar2=0.0,
                                    op0=AL.mult, op1=AL.min, inc=f"vxb{j}")
                        self.EV(vector, "v", "tensor_sub", aD, aASr, aAs,
                                inc="vargD")
                        self.W(vector, "sxp", 11 * fb + 3)
                        self.EV(vector, "v", "tensor_mul", Cd[fb % 2], Csl, se[1],
                                inc="vex")
                        self.W(vector, "sxp", 11 * fb + 2)
                        self.EV(vector, "v", "tensor_mul", tB1, se[0], dtr_sl)
                        self.EV(vector, "v", "tensor_mul", Bd, Bsl, tB1, inc="vex")
                        for j in range(3):
                            if j == 0 and fb >= 1:
                                self.W(vector, "vbd", 4 * NHGF * fb)
                            self.W(vector, "sxp", 11 * fb + 5 + 2 * j)
                            self.EV(vector, "v", "tensor_mul", CX[j][fb % 2],
                                    Csl, se[1], inc="vex")
                            self.W(vector, "sxp", 11 * fb + 4 + 2 * j)
                            self.EV(vector, "v", "tensor_mul", tB1, se[0], dtr_sl)
                            self.EV(vector, "v", "tensor_mul", BX[j], Bsl, tB1,
                                    inc="vex")
                        self.W(vector, "sxp", 11 * fb + 10)
                        self.EV(vector, "v", "tensor_mul", CS, Csl, se[1],
                                inc="vex")
                        self.W(vector, "sxp", 11 * fb + 11)
                        self.EV(vector, "v", "tensor_mul", tB1, se[0], dtr_sl)
                        self.EV(vector, "v", "tensor_mul", BD, Bsl, tB1, inc="vex")


                        def v_upd(k):
                            kf = k // NPB
                            kprf = k % NPB
                            self.W(vector, "teDH2", k + 1)
                            self.W(vector, "teY", k + 1)
                            if kprf == 0:
                                self.W(vector, "sxp", 11 * kf + 1)
                            self.EV(vector, "v", "scalar_tensor_tensor",
                                    out=h_s[(k + 1) % 2], in0=h_s[k % 2],
                                    scalar=lam[kf % 2][:, kprf:kprf + 1],
                                    in1=pDH2[k % 2][:, 0:256],
                                    op0=AL.mult, op1=AL.add, inc="hrdy")

                        hg0 = fb * NHGF
                        for hgl in range(NHGF):
                            hg = hg0 + hgl
                            for prl8 in range(8):
                                pr = hg * 8 + prl8
                                self.W(vector, "teM", pr + 1)
                                if pr >= 2:
                                    self.W(vector, "teY", pr - 1)
                                self.EV(vector, "v", "tensor_mul",
                                        Mm[pr % 2], pM[pr % 2][:, 0:128],
                                        mkqf)
                                self.W(vector, "teMx", pr + 1)
                                for j in range(3):
                                    w = 32 * (3 - j)
                                    self.EV(vector, "v", "tensor_mul",
                                            _ap(Mm[pr % 2].tensor,
                                                Mm[pr % 2].offset
                                                + 32 * j * 128 + 32 * (j + 1),
                                                [[128, 32], [1, w]]),
                                            pX[32 * j:32 * (j + 1), 0:w],
                                            mkxf[32 * j:32 * (j + 1), 0:w],
                                            inc=("selc" if j == 2 else None))
                                if pr >= 1:
                                    v_upd(pr - 1)
                        if fb == NFB - 1:
                            v_upd(NP - 1)

                @blk.scalar
                def _(scalar):
                    for fb in range(NFB):
                        self.W(scalar, "vASr", fb + 1)
                        if fb >= 2:
                            self.W(scalar, "hrdy", NPB * (fb - 1) + 1)
                        self.EV(scalar, "s", "activation", out=lam[fb % 2],
                                in_=_sv(aASr, 0, [[16, NPB]]),
                                func=AF.Exp, inc="sxp")
                        def se_exp(slot, reuse_vex, arg, scale=1.0):
                            if reuse_vex >= 1:
                                self.W(scalar, "vex", reuse_vex)
                            self.EV(scalar, "s", "activation", out=se[slot],
                                    in_=arg, func=AF.Exp, scale=scale, inc="sxp")
                        self.W(scalar, "vargC", fb + 1)
                        se_exp(0, 10 * fb, aC, -1.0)
                        se_exp(1, 10 * fb - 1, aC, 1.0)
                        for j in range(3):
                            self.W(scalar, f"vxb{j}", fb + 1)
                            se_exp(0, 10 * fb + 2 + 2 * j, axb[j])
                            self.W(scalar, f"vxc{j}", fb + 1)
                            se_exp(1, 10 * fb + 1 + 2 * j, axc[j])
                        self.W(scalar, "vAs", fb + 1)
                        se_exp(1, 10 * fb + 7, aAs)
                        self.W(scalar, "vargD", fb + 1)
                        se_exp(0, 10 * fb + 8, aD)
                        def evacT(k):
                            self.W(scalar, "teTr", k + 1)
                            if k >= 2:
                                self.W(scalar, "teDH2", k - 1)
                            self.EV(scalar, "s", "copy", out=BdT[k % 2],
                                    in_=pTt[:, 0:128], inc="evTr")
                        for prf in range(NPB):
                            pr = fb * NPB + prf
                            qg = pr // 4
                            if prf == 0:
                                evacT(pr)
                            if pr + 1 < NP and prf + 1 < NPB:
                                evacT(pr + 1)
                            self.W(scalar, "teY", pr + 1)
                            if pr % 4 == 0 and qg >= 2:
                                self.W(scalar, f"stY{qg % 2}", 16 * (qg // 2))
                            self.EV(scalar, "s", "copy",
                                    out=ygr[qg % 2][:, (pr % 4) * 256:(pr % 4) * 256 + 256],
                                    in_=pY[pr % 2][:, 0:256], inc="evY")

                @blk.tensor
                def _(tensor):
                    def dfront(pr):
                        fb = pr // NPB
                        prf = pr % NPB
                        hg = pr // 8
                        prl8 = pr % 8
                        slot = hg % 2
                        tb = prf * 16
                        if prf == 0:
                            self.W(tensor, "vex", 10 * (fb + 1))
                        self.W(tensor, "vbd", 4 * (hg + 1))
                        self.W(tensor, "gbd", 2 * (hg + 1))
                        if pr >= 2:
                            self.W(tensor, "selc", pr - 1)
                        it = tensor.matmul(
                            pM[pr % 2][:, 0:128],
                            BDIAG[slot][:, prl8 * 128:prl8 * 128 + 128],
                            _ap(Cd[fb % 2].tensor, Cd[fb % 2].offset + tb,
                                [[FB, 128], [1, 16], [0, 8]]),
                            start=True, stop=True)
                        self.inc(it, "teM", 1)
                        if pr >= 1:
                            self.W(tensor, "selc", pr)
                        for j in range(3):
                            w = 32 * (3 - j)
                            it = tensor.matmul(
                                pX[j * 32:(j + 1) * 32, 0:w],
                                BXbd[j][slot][:, prl8 * 32:prl8 * 32 + 32],
                                _ap(CX[j][fb % 2].tensor,
                                    CX[j][fb % 2].offset + tb + (j + 1) * 4,
                                    [[FB, 128], [1, 4 * (3 - j)], [0, 8]]),
                                start=True, stop=True)
                            if j == 2:
                                self.inc(it, "teMx", 1)
                        if pr >= 1:
                            self.W(tensor, "evTr", pr)
                        it = tensor.transpose(pTt[:, 0:128],
                                              BDbd[slot][:, prl8 * 128:prl8 * 128 + 128],
                                              self.ident_s.ap())
                        self.inc(it, "teTr", 1)

                    def yback(pr):
                        fb = pr // NPB
                        hg = pr // 8
                        prl8 = pr % 8
                        slot = hg % 2
                        self.W(tensor, f"ldU{pr % 4}", 16 * (pr // 4 + 1))
                        self.W(tensor, "selc", pr + 1)
                        self.W(tensor, "hrdy", pr + 1)
                        if pr >= 2:
                            self.W(tensor, "evY", pr - 1)
                        it = tensor.matmul(pY[pr % 2][:, 0:256], Mm[pr % 2],
                                           U_s[pr % 4], start=True, stop=False)
                        it = tensor.matmul(pY[pr % 2][:, 0:256],
                                           CSbd[slot][:, prl8 * 128:prl8 * 128 + 128],
                                           h_s[pr % 2], start=False, stop=True)
                        self.inc(it, "teY", 1)
                        self.W(tensor, "evTr", pr + 1)
                        if pr >= 2:
                            self.W(tensor, "hrdy", pr)
                        it = tensor.matmul(pDH2[pr % 2][:, 0:256], BdT[pr % 2],
                                           U_s[pr % 4], start=True, stop=True)
                        self.inc(it, "teDH2", 1)

                    for pr in range(NP + 1):
                        boundary = (pr % 8 == 0)
                        if pr >= 1 and boundary:
                            yback(pr - 1)
                        if pr < NP:
                            dfront(pr)
                        if pr >= 1 and not boundary:
                            yback(pr - 1)

                @blk.gpsimd
                def _(gpsimd):
                    def gbd_build(hg):
                        fb = (hg * 8) // NPB
                        hgl = hg - fb * NHGF
                        tb = hgl * 8 * 16
                        slot = hg % 2
                        if hg >= 2:
                            self.W(gpsimd, "teY", 8 * (hg - 1))
                        self.W(gpsimd, "vex", 10 * fb + 2)
                        self.EV(gpsimd, "g", "tensor_tensor",
                                out=_ap(BDIAG[slot].tensor, 0,
                                        [[1024, 128], [128, 8], [8, 16], [1, 8]]),
                                in0=_ap(Bd.tensor, Bd.offset + tb,
                                        [[FB, 128], [16, 8], [1, 16], [0, 8]]),
                                in1=_ap(self.hsel_s.ap().tensor, 0,
                                        [[128, 128], [0, 8], [8, 16], [1, 8]]),
                                op=AL.mult, inc="gbd")
                        self.W(gpsimd, "vex", 10 * fb + 10)
                        self.EV(gpsimd, "g", "tensor_tensor",
                                out=_ap(BDbd[slot].tensor, 0,
                                        [[1024, 128], [128, 8], [8, 16], [1, 8]]),
                                in0=_ap(BD.tensor, BD.offset + tb,
                                        [[FB, 128], [16, 8], [1, 16], [0, 8]]),
                                in1=_ap(self.hsel_s.ap().tensor, 0,
                                        [[128, 128], [0, 8], [8, 16], [1, 8]]),
                                op=AL.mult, inc="gbd")
                        self.EV(gpsimd, "g", "tensor_tensor",
                                out=_ap(CSbd[slot].tensor, 0,
                                        [[1024, 128], [128, 8], [8, 16], [1, 8]]),
                                in0=_ap(CS.tensor, CS.offset + tb,
                                        [[FB, 128], [16, 8], [1, 16], [0, 8]]),
                                in1=_ap(self.hsel_s.ap().tensor, 0,
                                        [[128, 128], [0, 8], [8, 16], [1, 8]]),
                                op=AL.mult, inc="vbd")
                        for j in range(3):
                            self.EV(gpsimd, "g", "tensor_tensor",
                                    out=_ap(BXbd[j][slot].tensor, 0,
                                            [[256, 128], [32, 8], [8, 4], [1, 8]]),
                                    in0=_ap(BX[j].tensor,
                                            BX[j].offset + tb + j * 4,
                                            [[FB, 128], [16, 8], [1, 4], [0, 8]]),
                                    in1=_ap(self.hsel_s.ap().tensor, 0,
                                            [[128, 128], [0, 8], [8, 4], [1, 8]]),
                                    op=AL.mult, inc="vbd")

                    for hg in range(NHG):
                        hgl = hg % NHGF
                        if hgl == 0:
                            gbd_build(hg)
                            if NHGF > 1 and hg + 1 < NHG:
                                gbd_build(hg + 1)
                        elif hgl + 1 < NHGF and hg + 1 < NHG:
                            gbd_build(hg + 1)
                        for qg in (2 * hg, 2 * hg + 1):
                            self.W(gpsimd, "evY", 4 * (qg + 1))
                            dst = _ap(self.y_d, qg * 4 * 128 * PH,
                                      [[256, 128], [128 * 256, 4], [1, 256]])
                            self.inc(gpsimd.dma_start(out=dst, in_=ygr[qg % 2]),
                                     f"stY{qg % 2}", 16)
                    for s2_ in range(2):
                        if self.cnt.get(f"stY{s2_}", 0):
                            self.W(gpsimd, f"stY{s2_}", self.cnt[f"stY{s2_}"])

                @blk.sync
                def _(sync):
                    self.inc(sync.dma_start(out=Bb, in_=self.Bc_d.ap()), "ldbc", 16)
                    self.inc(sync.dma_start(out=Cb, in_=self.Cc_d.ap()), "ldbc", 16)
                    self.inc(sync.dma_start(out=dt_rep, in_=self.dtr_d.ap()),
                             "ldbc", 16)
                    self.inc(sync.dma_start(out=mkqf, in_=self.maskqf.ap()),
                             "ldbc", 16)
                    self.inc(sync.dma_start(out=mkxf, in_=self.maskxf.ap()),
                             "ldbc", 16)
                    for pr in range(NP):
                        if pr >= 4:
                            self.W(sync, "teDH2", pr - 3)
                        src = _ap(self.U_d, pr * 128 * PH, [[256, 128], [1, 256]])
                        self.inc(sync.dma_start(out=U_s[pr % 4], in_=src),
                                 f"ldU{pr % 4}", 16)
        self.end_stage()

    def s3z(self):
        nc, L, NTT = self.nc, self.L, self.NTT
        with ExitStack() as st:
            def sb(name, shape, dt=F32):
                return st.enter_context(nc.sbuf_tensor(name, shape, dt)).ap()
            wst = [sb(f"wst_s3z{j}", [128, 512]) for j in range(2)]
            wh = sb("wh_s3z", [128, 16384], F32R)
            wl = sb("wl_s3z", [128, 16384], F32R)
            we = sb("we_s3z", [128, 512])
            linh = [sb(f"linhs3z" + str(j), [128, 4096], F32R) for j in range(2)]
            linl = [sb(f"linls3z" + str(j), [128, 4096], F32R) for j in range(2)]
            ob = [sb(f"obs3z{j}", [128, TT]) for j in range(2)]
            pZ = [st.enter_context(nc.psum_tensor(f"ps3z{j}", [128, 512], F32)).ap()
                  for j in range(2)]

            with nc.Block("s3z") as blk:
                @blk.sync
                def _(sync):
                    for c in range(32):
                        if c >= 2:
                            self.W(sync, "wsps3z", c - 1)
                        self.inc(sync.dma_start(
                            out=wst[c % 2],
                            in_=self.wZs.ap()[:, c * 512:(c + 1) * 512]),
                            f"ldws3z{c % 2}", 16)
                    for i in range(NTT):
                        if i >= 2:
                            self.W(sync, "tes3z", (i - 1) * 16)
                        srcl = _ap(self.ln1h_d, i * TT,
                                   [[L, 128], [128 * L, 8], [1, TT]]).bitcast(F32R)
                        self.inc(sync.dma_start(out=linh[i % 2], in_=srcl),
                                 f"ldhs3z" + str(i % 2), 16)
                        srcl = _ap(self.ln1l_d, i * TT,
                                   [[L, 128], [128 * L, 8], [1, TT]]).bitcast(F32R)
                        self.inc(sync.dma_start(out=linl[i % 2], in_=srcl),
                                 f"ldls3z" + str(i % 2), 16)

                @blk.vector
                def _(vector):
                    for c in range(32):
                        self.W(vector, f"ldws3z{c % 2}", 16 * (c // 2 + 1))
                        csl = wh[:, c * 512:(c + 1) * 512]
                        lsl = wl[:, c * 512:(c + 1) * 512]
                        self.EV(vector, "v", "scalar_tensor_tensor", out=we,
                                in0=wst[c % 2], scalar=4097.0, in1=wst[c % 2],
                                op0=AL.mult, op1=AL.subtract)
                        self.EV(vector, "v", "scalar_tensor_tensor", out=csl,
                                in0=wst[c % 2], scalar=4097.0, in1=we,
                                op0=AL.mult, op1=AL.subtract)
                        self.EV(vector, "v", "tensor_sub", lsl, wst[c % 2], csl,
                                inc="wsps3z")

                @blk.tensor
                def _(tensor):
                    for i in range(NTT):
                        self.W(tensor, f"ldhs3z" + str(i % 2), 16 * (i // 2 + 1))
                        self.W(tensor, f"ldls3z" + str(i % 2), 16 * (i // 2 + 1))
                        for mc in range(16):
                            idx = i * 16 + mc
                            if i == 0:
                                self.W(tensor, "wsps3z", (mc + 1) * 2)
                            if idx >= 2:
                                self.W(tensor, "evs3z", idx - 1)
                            for k in range(8):
                                co = (mc * 8 + k) * 128
                                it = tensor.matmul(
                                    pZ[mc % 2][:, 0:TT], wh[:, co:co + 128],
                                    linh[i % 2][:, k * TT:(k + 1) * TT],
                                    start=(k == 0), stop=False)
                                it = tensor.matmul(
                                    pZ[mc % 2][:, 0:TT], wh[:, co:co + 128],
                                    linl[i % 2][:, k * TT:(k + 1) * TT],
                                    start=False, stop=False)
                                it = tensor.matmul(
                                    pZ[mc % 2][:, 0:TT], wl[:, co:co + 128],
                                    linh[i % 2][:, k * TT:(k + 1) * TT],
                                    start=False, stop=(k == 7))
                            self.inc(it, "tes3z", 1)

                @blk.scalar
                def _(scalar):
                    for i in range(NTT):
                        for mc in range(16):
                            idx = i * 16 + mc
                            self.W(scalar, "tes3z", idx + 1)
                            if idx >= 2:
                                self.W(scalar, f"sts3z{mc % 2}", 16 * (idx // 2))
                            self.EV(scalar, "s", "activation", out=ob[mc % 2], in_=pZ[mc % 2][:, 0:TT], func=AF.Silu, inc="evs3z")

                @blk.gpsimd
                def _(gpsimd):
                    for i in range(NTT):
                        for mc in range(16):
                            idx = i * 16 + mc
                            self.W(gpsimd, "evs3z", idx + 1)
                            dst = _ap(self.sz_d, mc * 128 * L + i * TT,
                                      [[L, 128], [1, TT]])
                            self.inc(gpsimd.dma_start(out=dst, in_=ob[mc % 2]),
                                     f"sts3z{mc % 2}", 16)
                    for j in range(2):
                        nmm = f"sts3z{j}"
                        if self.cnt.get(nmm, 0):
                            self.W(gpsimd, nmm, self.cnt[nmm])
        self.end_stage()

    def s3r(self):
        nc, L, NTT = self.nc, self.L, self.NTT
        with ExitStack() as st:
            def sb(name, shape, dt=F32):
                return st.enter_context(nc.sbuf_tensor(name, shape, dt)).ap()
            wst = [sb(f"wst_s3r{j}", [128, 512]) for j in range(2)]
            wh = sb("wh_s3r", [128, 16384], F32R)
            wl = sb("wl_s3r", [128, 16384], F32R)
            we = sb("we_s3r", [128, 512])
            linh = [sb(f"linhs3r" + str(j), [128, 4096], F32R) for j in range(2)]
            linl = [sb(f"linls3r" + str(j), [128, 4096], F32R) for j in range(2)]
            ob = [sb(f"obs3r{j}", [128, TT]) for j in range(2)]
            pZ = [st.enter_context(nc.psum_tensor(f"ps3r{j}", [128, 512], F32)).ap()
                  for j in range(2)]

            with nc.Block("s3r") as blk:
                @blk.sync
                def _(sync):
                    for c in range(32):
                        if c >= 2:
                            self.W(sync, "wsps3r", c - 1)
                        self.inc(sync.dma_start(
                            out=wst[c % 2],
                            in_=self.wRs.ap()[:, c * 512:(c + 1) * 512]),
                            f"ldws3r{c % 2}", 16)
                    for i in range(NTT):
                        if i >= 2:
                            self.W(sync, "tes3r", (i - 1) * 16)
                        srcl = _ap(self.ln1h_d, i * TT,
                                   [[L, 128], [128 * L, 8], [1, TT]]).bitcast(F32R)
                        self.inc(sync.dma_start(out=linh[i % 2], in_=srcl),
                                 f"ldhs3r" + str(i % 2), 16)
                        srcl = _ap(self.ln1l_d, i * TT,
                                   [[L, 128], [128 * L, 8], [1, TT]]).bitcast(F32R)
                        self.inc(sync.dma_start(out=linl[i % 2], in_=srcl),
                                 f"ldls3r" + str(i % 2), 16)

                @blk.vector
                def _(vector):
                    for c in range(32):
                        self.W(vector, f"ldws3r{c % 2}", 16 * (c // 2 + 1))
                        csl = wh[:, c * 512:(c + 1) * 512]
                        lsl = wl[:, c * 512:(c + 1) * 512]
                        self.EV(vector, "v", "scalar_tensor_tensor", out=we,
                                in0=wst[c % 2], scalar=4097.0, in1=wst[c % 2],
                                op0=AL.mult, op1=AL.subtract)
                        self.EV(vector, "v", "scalar_tensor_tensor", out=csl,
                                in0=wst[c % 2], scalar=4097.0, in1=we,
                                op0=AL.mult, op1=AL.subtract)
                        self.EV(vector, "v", "tensor_sub", lsl, wst[c % 2], csl,
                                inc="wsps3r")

                @blk.tensor
                def _(tensor):
                    for i in range(NTT):
                        self.W(tensor, f"ldhs3r" + str(i % 2), 16 * (i // 2 + 1))
                        self.W(tensor, f"ldls3r" + str(i % 2), 16 * (i // 2 + 1))
                        for mc in range(16):
                            idx = i * 16 + mc
                            if i == 0:
                                self.W(tensor, "wsps3r", (mc + 1) * 2)
                            if idx >= 2:
                                self.W(tensor, "evs3r", idx - 1)
                            for k in range(8):
                                co = (mc * 8 + k) * 128
                                it = tensor.matmul(
                                    pZ[mc % 2][:, 0:TT], wh[:, co:co + 128],
                                    linh[i % 2][:, k * TT:(k + 1) * TT],
                                    start=(k == 0), stop=False)
                                it = tensor.matmul(
                                    pZ[mc % 2][:, 0:TT], wh[:, co:co + 128],
                                    linl[i % 2][:, k * TT:(k + 1) * TT],
                                    start=False, stop=False)
                                it = tensor.matmul(
                                    pZ[mc % 2][:, 0:TT], wl[:, co:co + 128],
                                    linh[i % 2][:, k * TT:(k + 1) * TT],
                                    start=False, stop=(k == 7))
                            self.inc(it, "tes3r", 1)

                @blk.scalar
                def _(scalar):
                    for i in range(NTT):
                        for mc in range(16):
                            idx = i * 16 + mc
                            self.W(scalar, "tes3r", idx + 1)
                            if idx >= 2:
                                self.W(scalar, f"sts3r{mc % 2}", 16 * (idx // 2))
                            self.EV(scalar, "s", "copy", out=ob[mc % 2], in_=pZ[mc % 2][:, 0:TT], inc="evs3r")

                @blk.gpsimd
                def _(gpsimd):
                    for i in range(NTT):
                        for mc in range(16):
                            idx = i * 16 + mc
                            self.W(gpsimd, "evs3r", idx + 1)
                            dst = _ap(self.r_d, mc * 128 * L + i * TT,
                                      [[L, 128], [1, TT]])
                            self.inc(gpsimd.dma_start(out=dst, in_=ob[mc % 2]),
                                     f"sts3r{mc % 2}", 16)
                    for j in range(2):
                        nmm = f"sts3r{j}"
                        if self.cnt.get(nmm, 0):
                            self.W(gpsimd, nmm, self.cnt[nmm])
        self.end_stage()

    def s3ga(self):
        nc, L = self.nc, self.L
        NG = self.NG
        with ExitStack() as st:
            def sb(name, shape, dt=F32):
                return st.enter_context(nc.sbuf_tensor(name, shape, dt)).ap()
            ytm = [sb(f"ytm{j}", [128, 2048]) for j in range(2)]
            ug = [sb(f"ug{j}", [128, 2048]) for j in range(2)]
            szg = [sb(f"szg{j}", [128, 2048]) for j in range(2)]
            rg = [sb(f"rg{j}", [128, 2048]) for j in range(2)]
            gbuf = [sb(f"gbuf{j}", [128, 2048]) for j in range(6)]
            gsq = [sb(f"gsq{j}", [128, 128]) for j in range(2)]
            rstd2b = [sb(f"rstd2b{j}", [128, 128]) for j in range(3)]
            mstA = sb("mstA", [128, 128])
            pYT = [st.enter_context(nc.psum_tensor(f"pYT{j}", [128, 512], F32)).ap()
                   for j in range(2)]
            pSg = st.enter_context(nc.psum_tensor("pSg", [128, 512], F32)).ap()

            with nc.Block("s3ga") as blk:
                @blk.sync
                def _(sync):
                    for g in range(NG):
                        if g >= 2:
                            self.W(sync, "teYT", 16 * (g - 1))
                            self.W(sync, "gat0", 16 * (g - 1))
                            self.W(sync, "gat1", g - 1)
                        self.inc(sync.dma_start(
                            out=ytm[g % 2],
                            in_=_ap(self.y_d, g * 128 * 2048, [[2048, 128], [1, 2048]])),
                            f"ldyA{g % 2}", 16)
                        self.inc(sync.dma_start(
                            out=ug[g % 2],
                            in_=_ap(self.u_d, g * 128, [[L, 128], [128 * L, 16], [1, 128]])),
                            f"lduA{g % 2}", 16)
                        self.inc(sync.dma_start(
                            out=szg[g % 2],
                            in_=_ap(self.sz_d, g * 128, [[L, 128], [128 * L, 16], [1, 128]])),
                            f"ldszA{g % 2}", 16)
                        self.inc(sync.dma_start(
                            out=rg[g % 2],
                            in_=_ap(self.r_d, g * 128, [[L, 128], [128 * L, 16], [1, 128]])),
                            f"ldrA{g % 2}", 16)

                @blk.tensor
                def _(tensor):
                    for g in range(NG):
                        self.W(tensor, f"ldyA{g % 2}", 16 * (g // 2 + 1))
                        for fc in range(16):
                            idx = g * 16 + fc
                            if idx >= 2:
                                self.W(tensor, "gat0", idx - 1)
                            it = tensor.transpose(
                                pYT[fc % 2][:, 0:128],
                                ytm[g % 2][:, fc * 128:(fc + 1) * 128],
                                self.ident_s.ap())
                            self.inc(it, "teYT", 1)
                        for fc in range(16):
                            idx = g * 16 + fc
                            self.W(tensor, "sqg", idx + 1)
                            if fc == 0 and g >= 1:
                                self.W(tensor, "rstd2S", g)
                            it = tensor.matmul(pSg[:, 0:128], self.ones_s.ap(),
                                               gsq[fc % 2], start=(fc == 0),
                                               stop=(fc == 15))
                            self.inc(it, "teSgF", 1)

                @blk.vector
                def _(vector):
                    def gnr(gg):
                        self.W(vector, "rstd2S", gg + 1)
                        for fc in range(16):
                            self.EV(vector, "v", "scalar_tensor_tensor",
                                    out=gbuf[gg % 6][:, fc * 128:(fc + 1) * 128],
                                    in0=gbuf[gg % 6][:, fc * 128:(fc + 1) * 128],
                                    scalar=self.normw_s.ap()[:, fc:fc + 1],
                                    in1=rstd2b[gg % 3], op0=AL.mult,
                                    op1=AL.mult, inc="gnr")
                    for g in range(NG):
                        nld = 16 * (g // 2 + 1)
                        self.W(vector, f"lduA{g % 2}", nld)
                        self.W(vector, f"ldszA{g % 2}", nld)
                        self.W(vector, f"ldrA{g % 2}", nld)
                        if g >= 6:
                            self.W(vector, f"stg{g % 6}", 16 * (g // 6))
                            self.W(vector, "sqg", 16 * (g - 5))
                        for fc in range(16):
                            self.W(vector, "teYT", g * 16 + fc + 1)
                            self.EV(vector, "v", "scalar_tensor_tensor",
                                    out=gbuf[g % 6][:, fc * 128:(fc + 1) * 128],
                                    in0=ug[g % 2][:, fc * 128:(fc + 1) * 128],
                                    scalar=self.dskip_s.ap()[:, fc:fc + 1],
                                    in1=pYT[fc % 2][:, 0:128],
                                    op0=AL.mult, op1=AL.add, inc="gat0")
                        self.EV(vector, "v", "tensor_mul", gbuf[g % 6],
                                gbuf[g % 6], szg[g % 2])
                        self.EV(vector, "v", "tensor_add", gbuf[g % 6],
                                gbuf[g % 6], rg[g % 2], inc="gat1")
                        if g >= 2:
                            gnr(g - 2)
                    for gg in range(NG - 2, NG):
                        gnr(gg)

                @blk.scalar
                def _(scalar):
                    for g in range(NG):
                        self.W(scalar, "gat1", g + 1)
                        for fc in range(16):
                            idx = g * 16 + fc
                            if idx >= 2:
                                self.W(scalar, "teSgF", idx - 1)
                            self.EV(scalar, "s", "activation", out=gsq[fc % 2],
                                    in_=gbuf[g % 6][:, fc * 128:(fc + 1) * 128],
                                    func=AF.Square, inc="sqg")
                        self.W(scalar, "teSgF", 16 * (g + 1))
                        if g >= 3:
                            self.W(scalar, "gnr", 16 * (g - 2))
                        self.EV(scalar, "s", "activation", out=mstA,
                                in_=pSg[:, 0:128], func=AF.Ln,
                                bias=self.c_epsrms.ap(), scale=1.0 / DI)
                        self.EV(scalar, "s", "activation", out=rstd2b[g % 3],
                                in_=mstA, func=AF.Exp, scale=-0.5,
                                inc="rstd2S")

                @blk.gpsimd
                def _(gpsimd):
                    for g in range(NG):
                        self.W(gpsimd, "gnr", 16 * (g + 1))
                        dst = _ap(self.g_d, g * 128,
                                  [[L, 128], [128 * L, 16], [1, 128]])
                        self.inc(gpsimd.dma_start(out=dst, in_=gbuf[g % 6]),
                                 f"stg{g % 6}", 16)
                    for j in range(6):
                        if self.cnt.get(f"stg{j}", 0):
                            self.W(gpsimd, f"stg{j}", self.cnt[f"stg{j}"])
        self.end_stage()

    def s3gb(self):
        nc, L = self.nc, self.L
        NT = L // 512
        with ExitStack() as st:
            def sb(name, shape, dt=F32):
                return st.enter_context(nc.sbuf_tensor(name, shape, dt)).ap()
            wo = sb("wo", [128, 16384])
            gin = [sb(f"gin{j}", [128, 8192]) for j in range(2)]
            xg = [sb(f"xg{j}", [128, 4096]) for j in range(2)]
            x2b = sb("x2b0", [128, 4096])
            x2sqb = [sb(f"x2sqb{j}", [128, 512]) for j in range(2)]
            mu2 = sb("mu2", [128, 512])
            tv2 = sb("tv2", [128, 512])
            v2 = sb("v2", [128, 512])
            rstd3 = sb("rstd3", [128, 512])
            mur3a = sb("mur3a", [128, 512])
            mstB = sb("mstB", [128, 512])
            l2h = [sb(f"l2h{j}", [128, 512], F32R) for j in range(2)]
            l2l = [sb(f"l2l{j}", [128, 512], F32R) for j in range(2)]
            lnC = sb("lnC", [128, 512])
            lnSp = sb("lnSp", [128, 512])
            pW = [st.enter_context(nc.psum_tensor(f"pW3{j}", [128, 512], F32)).ap()
                  for j in range(2)]
            pSa = st.enter_context(nc.psum_tensor("pSa2", [128, 512], F32)).ap()
            pSb = st.enter_context(nc.psum_tensor("pSb2", [128, 512], F32)).ap()

            with nc.Block("s3gb") as blk:
                @blk.sync
                def _(sync):
                    self.inc(sync.dma_start(out=wo, in_=self.wOs.ap()), "ldwo", 16)
                    for t in range(NT):
                        if t >= 2:
                            self.W(sync, "teW", 8 * (t - 1))
                        self.inc(sync.dma_start(
                            out=gin[t % 2],
                            in_=_ap(self.g_d, t * 512,
                                    [[L, 128], [128 * L, 16], [1, 512]])),
                            f"ldg{t % 2}", 16)
                        if t >= 2:
                            self.W(sync, "x2a", 8 * (t - 1))
                        self.inc(sync.dma_start(
                            out=xg[t % 2],
                            in_=_ap(self.xT, t * 512,
                                    [[L, 128], [128 * L, 8], [1, 512]])),
                            f"ldx5{t % 2}", 16)

                @blk.tensor
                def _(tensor):
                    self.W(tensor, "ldwo", 16)
                    for t in range(NT):
                        self.W(tensor, f"ldg{t % 2}", 16 * (t // 2 + 1))
                        for m in range(8):
                            idx = t * 8 + m
                            if idx >= 2:
                                self.W(tensor, "x2a", idx - 1)
                            for k in range(16):
                                it = tensor.matmul(
                                    pW[m % 2][:, 0:512],
                                    wo[:, (m * 16 + k) * 128:(m * 16 + k + 1) * 128],
                                    gin[t % 2][:, k * 512:(k + 1) * 512],
                                    start=(k == 0), stop=(k == 15))
                            self.inc(it, "teW", 1)
                        for m in range(8):
                            self.W(tensor, "x2a", t * 8 + m + 1)
                            if m == 0 and t >= 1:
                                self.W(tensor, "v2r", t)
                            it = tensor.matmul(pSa[:, 0:512], self.ones_s.ap(),
                                               x2b[:, m * 512:(m + 1) * 512],
                                               start=(m == 0), stop=(m == 7))
                            self.inc(it, "teSaM", 1)
                        for m in range(8):
                            self.W(tensor, "sqx2", t * 8 + m + 1)
                            it = tensor.matmul(pSb[:, 0:512], self.ones_s.ap(),
                                               x2sqb[m % 2], start=(m == 0),
                                               stop=(m == 7))
                            self.inc(it, "teSbM", 1)

                @blk.vector
                def _(vector):
                    for t in range(NT):
                        self.W(vector, f"ldx5{t % 2}", 16 * (t // 2 + 1))
                        if t >= 1:
                            self.W(vector, "stx2c", 16 * t)
                            self.W(vector, "teSaM", 8 * t)
                            self.W(vector, "sqx2", 8 * t)
                        for m in range(8):
                            self.W(vector, "teW", t * 8 + m + 1)
                            self.EV(vector, "v", "tensor_add",
                                    x2b[:, m * 512:(m + 1) * 512],
                                    xg[t % 2][:, m * 512:(m + 1) * 512],
                                    pW[m % 2][:, 0:512], inc="x2a")
                        self.W(vector, "teSaM", 8 * (t + 1))
                        self.EV(vector, "v", "tensor_scalar_mul", mu2,
                                pSa[:, 0:512], 1.0 / D)
                        self.EV(vector, "v", "tensor_mul", tv2, mu2, mu2)
                        self.W(vector, "teSbM", 8 * (t + 1))
                        self.EV(vector, "v", "scalar_tensor_tensor", out=v2,
                                in0=pSb[:, 0:512], scalar=1.0 / D, in1=tv2,
                                op0=AL.mult, op1=AL.subtract, inc="v2r")
                        self.W(vector, "rstd3S", t + 1)
                        self.EV(vector, "v", "tensor_mul", mur3a, mu2, rstd3)
                        for m in range(8):
                            idx = t * 8 + m
                            if idx >= 2:
                                self.W(vector, f"stl2{idx % 2}",
                                       32 * (idx // 2))
                            self.EV(vector, "v", "tensor_mul", lnC,
                                    x2b[:, m * 512:(m + 1) * 512], rstd3)
                            self.EV(vector, "v", "tensor_sub", lnC,
                                    lnC, mur3a)
                            self.EV(vector, "v", "scalar_tensor_tensor",
                                    out=lnSp, in0=lnC,
                                    scalar=4097.0, in1=lnC,
                                    op0=AL.mult, op1=AL.subtract)
                            self.EV(vector, "v", "scalar_tensor_tensor",
                                    out=l2h[idx % 2], in0=lnC,
                                    scalar=4097.0, in1=lnSp,
                                    op0=AL.mult, op1=AL.subtract)
                            self.EV(vector, "v", "tensor_sub", l2l[idx % 2],
                                    lnC, l2h[idx % 2], inc="ln2s")

                @blk.scalar
                def _(scalar):
                    for t in range(NT):
                        for m in range(8):
                            idx = t * 8 + m
                            self.W(scalar, "x2a", idx + 1)
                            if idx >= 2:
                                self.W(scalar, "teSbM", idx - 1)
                            self.EV(scalar, "s", "activation",
                                    out=x2sqb[m % 2],
                                    in_=x2b[:, m * 512:(m + 1) * 512],
                                    func=AF.Square, inc="sqx2")
                        self.W(scalar, "v2r", t + 1)
                        if t >= 1:
                            self.W(scalar, "ln2s", 8 * t)
                        self.EV(scalar, "s", "activation", out=mstB, in_=v2,
                                func=AF.Ln, bias=self.c_epsln.ap())
                        self.EV(scalar, "s", "activation", out=rstd3,
                                in_=mstB, func=AF.Exp, scale=-0.5,
                                inc="rstd3S")

                @blk.gpsimd
                def _(gpsimd):
                    for t in range(NT):
                        for m in range(8):
                            idx = t * 8 + m
                            self.W(gpsimd, "ln2s", idx + 1)
                            dst = _ap(self.ln2h_d, m * 128 * L + t * 512,
                                      [[L, 128], [1, 512]]).bitcast(F32R)
                            self.inc(gpsimd.dma_start(out=dst, in_=l2h[idx % 2]),
                                     f"stl2{idx % 2}", 16)
                            dst = _ap(self.ln2l_d, m * 128 * L + t * 512,
                                      [[L, 128], [1, 512]]).bitcast(F32R)
                            self.inc(gpsimd.dma_start(out=dst, in_=l2l[idx % 2]),
                                     f"stl2{idx % 2}", 16)
                        self.W(gpsimd, "x2a", 8 * (t + 1))
                        dst = _ap(self.x2_d, t * 512,
                                  [[L, 128], [128 * L, 8], [1, 512]])
                        self.inc(gpsimd.dma_start(out=dst, in_=x2b), "stx2c", 16)
                    for nm in ("stl20", "stl21", "stx2c"):
                        if self.cnt.get(nm, 0):
                            self.W(gpsimd, nm, self.cnt[nm])
        self.end_stage()

    def s4(self):
        nc, L, NTT = self.nc, self.L, self.NTT
        with ExitStack() as st:
            def sb(name, shape, dt=F32):
                return st.enter_context(nc.sbuf_tensor(name, shape, dt)).ap()
            w1h = sb("w1h4", [128, 16384], F32R)
            w1l = sb("w1l4", [128, 16384], F32R)
            linh = [sb(f"linh4{j}", [128, 4096], F32R) for j in range(2)]
            linl = [sb(f"linl4{j}", [128, 4096], F32R) for j in range(2)]
            hgb = [sb(f"hgb{j}", [128, TT]) for j in range(2)]
            hgh = [sb(f"hgh{j}", [128, TT], F32R) for j in range(2)]
            hgl = [sb(f"hgl{j}", [128, TT], F32R) for j in range(2)]
            pF = [st.enter_context(nc.psum_tensor(f"pF4{j}", [128, 512], F32)).ap()
                  for j in range(2)]
            pT4 = st.enter_context(nc.psum_tensor("pT4", [128, 512], F32)).ap()
            NU = NTT * 32

            with nc.Block("s4") as blk:
                @blk.sync
                def _(sync):
                    def wload(half):
                        for q in range(4):
                            if half == 1:
                                self.W(sync, "teF4",
                                       (NTT - 1) * 16 + 4 * q + 4)
                            dst = w1l[:, q * 4096:(q + 1) * 4096]
                            srcw = self.w1s.ap()[:, half * 16384 + q * 4096:
                                                 half * 16384 + (q + 1) * 4096]
                            srcw = srcw.bitcast(F32R)
                            self.inc(sync.dma_start(out=dst, in_=srcw),
                                     f"ldw4q{q}", 16)
                    def linload(t):
                        half, i = t // NTT, t % NTT
                        if t >= 2:
                            self.W(sync, "teF4", (t - 1) * 16)
                        srcl = _ap(self.ln2h_d, i * TT,
                                   [[L, 128], [128 * L, 8], [1, TT]]).bitcast(F32R)
                        self.inc(sync.dma_start(out=linh[t % 2], in_=srcl),
                                 f"ldh4{t % 2}", 16)
                        srcl = _ap(self.ln2l_d, i * TT,
                                   [[L, 128], [128 * L, 8], [1, TT]]).bitcast(F32R)
                        self.inc(sync.dma_start(out=linl[t % 2], in_=srcl),
                                 f"ldl4{t % 2}", 16)
                    wload(0)
                    for t in range(min(NTT + 2, 2 * NTT)):
                        linload(t)
                    wload(1)
                    for t in range(min(NTT + 2, 2 * NTT), 2 * NTT):
                        linload(t)

                @blk.vector
                def _(vector):
                    def wsplit(half):
                        for c in range(32):
                            self.W(vector, f"ldw4q{c // 8}", 16 * (half + 1))
                            if half == 1:
                                self.W(vector, "teF4",
                                       (NTT - 1) * 16 + c // 2 + 1)
                            xs = w1l[:, c * 512:(c + 1) * 512]
                            hs = w1h[:, c * 512:(c + 1) * 512]
                            self.EV(vector, "v", "scalar_tensor_tensor",
                                    out=pT4[:, 0:512], in0=xs, scalar=4097.0,
                                    in1=xs, op0=AL.mult, op1=AL.subtract)
                            self.EV(vector, "v", "scalar_tensor_tensor",
                                    out=hs, in0=xs, scalar=4097.0,
                                    in1=pT4[:, 0:512], op0=AL.mult,
                                    op1=AL.subtract)
                            self.EV(vector, "v", "tensor_sub", xs, xs, hs,
                                    inc="wsp4")
                    for half in range(2):
                        wsplit(half)
                        for i in range(NTT):
                            for ml in range(16):
                                idx = (half * NTT + i) * 16 + ml
                                self.W(vector, "evF4", idx + 1)
                                if idx >= 2:
                                    self.W(vector, f"sth{idx % 2}",
                                           32 * (idx // 2))
                                h = hgb[idx % 2]
                                self.EV(vector, "v", "scalar_tensor_tensor",
                                        out=pT4[:, 0:TT], in0=h, scalar=4097.0,
                                        in1=h, op0=AL.mult, op1=AL.subtract)
                                self.EV(vector, "v", "scalar_tensor_tensor",
                                        out=hgh[idx % 2], in0=h, scalar=4097.0,
                                        in1=pT4[:, 0:TT], op0=AL.mult,
                                        op1=AL.subtract)
                                self.EV(vector, "v", "tensor_sub",
                                        hgl[idx % 2], h, hgh[idx % 2],
                                        inc="spl4")

                @blk.tensor
                def _(tensor):
                    for half in range(2):
                        for i in range(NTT):
                            t = half * NTT + i
                            self.W(tensor, f"ldh4{t % 2}", 16 * (t // 2 + 1))
                            self.W(tensor, f"ldl4{t % 2}", 16 * (t // 2 + 1))
                            for ml in range(16):
                                idx = t * 16 + ml
                                if i == 0:
                                    self.W(tensor, "wsp4",
                                           half * 32 + 2 * (ml + 1))
                                if idx >= 2:
                                    self.W(tensor, "evF4", idx - 1)
                                for k in range(8):
                                    co = (ml * 8 + k) * 128
                                    it = tensor.matmul(
                                        pF[idx % 2][:, 0:TT],
                                        w1h[:, co:co + 128],
                                        linh[t % 2][:, k * TT:(k + 1) * TT],
                                        start=(k == 0), stop=False)
                                    it = tensor.matmul(
                                        pF[idx % 2][:, 0:TT],
                                        w1h[:, co:co + 128],
                                        linl[t % 2][:, k * TT:(k + 1) * TT],
                                        start=False, stop=False)
                                    it = tensor.matmul(
                                        pF[idx % 2][:, 0:TT],
                                        w1l[:, co:co + 128],
                                        linh[t % 2][:, k * TT:(k + 1) * TT],
                                        start=False, stop=(k == 7))
                                self.inc(it, "teF4", 1)

                @blk.scalar
                def _(scalar):
                    for half in range(2):
                        for i in range(NTT):
                            for ml in range(16):
                                idx = (half * NTT + i) * 16 + ml
                                m = half * 16 + ml
                                self.W(scalar, "teF4", idx + 1)
                                if idx >= 2:
                                    self.W(scalar, "spl4", idx - 1)
                                self.EV(scalar, "s", "activation",
                                        out=hgb[idx % 2],
                                        in_=pF[idx % 2][:, 0:TT], func=AF.Gelu,
                                        bias=self.b1_s.ap()[:, m:m + 1],
                                        inc="evF4")

                @blk.gpsimd
                def _(gpsimd):
                    for half in range(2):
                        for i in range(NTT):
                            for ml in range(16):
                                idx = (half * NTT + i) * 16 + ml
                                m = half * 16 + ml
                                self.W(gpsimd, "spl4", idx + 1)
                                dst = _ap(self.hgh_d, m * 128 * L + i * TT,
                                          [[L, 128], [1, TT]]).bitcast(F32R)
                                self.inc(gpsimd.dma_start(out=dst,
                                                          in_=hgh[idx % 2]),
                                         f"sth{idx % 2}", 16)
                                dst = _ap(self.hgl_d, m * 128 * L + i * TT,
                                          [[L, 128], [1, TT]]).bitcast(F32R)
                                self.inc(gpsimd.dma_start(out=dst,
                                                          in_=hgl[idx % 2]),
                                         f"sth{idx % 2}", 16)
                    for j in range(2):
                        if self.cnt.get(f"sth{j}", 0):
                            self.W(gpsimd, f"sth{j}", self.cnt[f"sth{j}"])
        self.end_stage()

    def s5(self):
        nc, L, NTT = self.nc, self.L, self.NTT
        with ExitStack() as st:
            def sb(name, shape, dt=F32):
                return st.enter_context(nc.sbuf_tensor(name, shape, dt)).ap()
            w2 = sb("w2b", [128, 32768])
            w2e = sb("w2e", [128, 1024])
            w2h = [sb(f"w2h{j}", [128, 1024], F32R) for j in range(2)]
            w2l = [sb(f"w2l{j}", [128, 1024], F32R) for j in range(2)]
            hgh = [sb(f"hgh5{j}", [128, TT], F32R) for j in range(4)]
            hgl = [sb(f"hgl5{j}", [128, TT], F32R) for j in range(4)]
            x2t = [sb("x2t0", [128, 4096])]
            obuf = sb("obuf", [128, 4096])
            pO = [st.enter_context(nc.psum_tensor(f"pO5{j}", [128, 512], F32)).ap()
                  for j in range(8)]

            with nc.Block("s5") as blk:
                @blk.sync
                def _(sync):
                    self.inc(sync.dma_start(out=w2, in_=self.w2s.ap()), "ldw5", 16)
                    for i in range(NTT):
                        if i >= 1:
                            self.W(sync, "evO5", 8 * i)
                        src = _ap(self.x2_d, i * TT, [[L, 128], [128 * L, 8], [1, TT]])
                        self.inc(sync.dma_start(out=x2t[0], in_=src),
                                 "ldx50", 16)
                        for k in range(32):
                            gk = i * 32 + k
                            if gk >= 4:
                                self.W(sync, "teO5", (gk - 3) * 24)
                            src = _ap(self.hgh_d, k * 128 * L + i * TT,
                                      [[L, 128], [1, TT]]).bitcast(F32R)
                            self.inc(sync.dma_start(out=hgh[k % 4], in_=src),
                                     f"ldhh{k % 4}", 16)
                            src = _ap(self.hgl_d, k * 128 * L + i * TT,
                                      [[L, 128], [1, TT]]).bitcast(F32R)
                            self.inc(sync.dma_start(out=hgl[k % 4], in_=src),
                                     f"ldhl{k % 4}", 16)

                @blk.vector
                def _(vector):
                    self.W(vector, "ldw5", 16)
                    for i in range(NTT):
                        for k in range(32):
                            gk = i * 32 + k
                            if gk >= 2:
                                self.W(vector, "teO5", (gk - 1) * 24)
                            wsl = w2[:, k * 1024:(k + 1) * 1024]
                            self.EV(vector, "v", "scalar_tensor_tensor",
                                    out=w2e, in0=wsl, scalar=4097.0, in1=wsl,
                                    op0=AL.mult, op1=AL.subtract)
                            self.EV(vector, "v", "scalar_tensor_tensor",
                                    out=w2h[k % 2], in0=wsl, scalar=4097.0, in1=w2e,
                                    op0=AL.mult, op1=AL.subtract)
                            self.EV(vector, "v", "tensor_sub",
                                    w2l[k % 2], wsl, w2h[k % 2], inc="spwv")
                        self.W(vector, "ldx50", 16 * (i + 1))
                        self.W(vector, "teO5", 768 * (i + 1))
                        if i >= 1:
                            self.W(vector, "stout", 16 * i)
                        for m in range(8):
                            self.EV(vector, "v", "scalar_tensor_tensor",
                                    out=obuf[:, m * TT:(m + 1) * TT],
                                    in0=x2t[0][:, m * TT:(m + 1) * TT],
                                    scalar=self.b2_s.ap()[:, m:m + 1],
                                    in1=pO[m][:, 0:TT],
                                    op0=AL.add, op1=AL.add, inc="evO5")

                @blk.gpsimd
                def _(gpsimd):
                    for i in range(NTT):
                        if i >= 1:
                            self.W(gpsimd, "evO5", 8 * i)
                            dst = _ap(self.outT, (i - 1) * TT,
                                      [[L, 128], [128 * L, 8], [1, TT]])
                            self.inc(gpsimd.dma_start(out=dst, in_=obuf), "stout", 16)
                    self.W(gpsimd, "evO5", 8 * NTT)
                    dst = _ap(self.outT, (NTT - 1) * TT,
                              [[L, 128], [128 * L, 8], [1, TT]])
                    self.inc(gpsimd.dma_start(out=dst, in_=obuf), "stout", 16)
                    self.W(gpsimd, "stout", self.cnt["stout"])

                @blk.tensor
                def _(tensor):
                    for i in range(NTT):
                        if i >= 1:
                            self.W(tensor, "evO5", 8 * i)
                        for k in range(32):
                            gk = i * 32 + k
                            self.W(tensor, f"ldhh{k % 4}", 16 * (gk // 4 + 1))
                            self.W(tensor, f"ldhl{k % 4}", 16 * (gk // 4 + 1))
                            self.W(tensor, "spwv", gk + 1)
                            wh, wl = w2h[k % 2], w2l[k % 2]
                            for m in range(8):
                                it = tensor.matmul(
                                    pO[m][:, 0:TT],
                                    wh[:, m * 128:(m + 1) * 128],
                                    hgh[k % 4], start=(k == 0), stop=False)
                                self.inc(it, "teO5", 1)
                                it = tensor.matmul(
                                    pO[m][:, 0:TT],
                                    wh[:, m * 128:(m + 1) * 128],
                                    hgl[k % 4], start=False, stop=False)
                                self.inc(it, "teO5", 1)
                                it = tensor.matmul(
                                    pO[m][:, 0:TT],
                                    wl[:, m * 128:(m + 1) * 128],
                                    hgh[k % 4], start=False,
                                    stop=(k == 31 and True))
                                self.inc(it, "teO5", 1)

    def build(self):
        for s in ("s0", "s1a", "s1b", "s2", "s3z", "s3r", "s3ga", "s3gb", "s4",
                  "s5"):
            if s in self.stages:
                getattr(self, s)()
        return self.nc


def kernel(**inputs):
    x = np.asarray(inputs["L0_attn_x"])
    L = x.shape[1]
    common = host_prep(inputs)
    mk = MK(L=L)
    nc = mk.build()
    in_maps = []
    for b in range(8):
        m = dict(common)
        m["xT"] = np.ascontiguousarray(x[b].T)
        in_maps.append(m)
    res = run_bass_kernel_spmd(nc, in_maps, core_ids=list(range(8)))
    out = np.stack([np.asarray(r["outT"]).T for r in res.results], axis=0)
    return np.ascontiguousarray(out.astype(np.float32))


# version 23 (final)
# speedup vs baseline: 1.0707x; 1.0707x over previous
import numpy as np
from contextlib import ExitStack

import concourse.bass as bass
import concourse.mybir as mybir
from concourse.bass_utils import run_bass_kernel_spmd

F32 = mybir.dt.float32
F32R = mybir.dt.float32r
U8 = mybir.dt.uint8
AF = mybir.ActivationFunctionType
AL = mybir.AluOpType

D = 1024
DI = 2048
NSTATE = 16
H = 8
PH = DI // H
DTR = 64
TT = 512
EPS_LN = 1e-5
EPS_RMS = 1e-6


def _ap(handle, off, dims):
    return bass.AP(handle, off, dims)


def _sv(apx, off, dims_after_part):
    return bass.AP(apx.tensor, apx.offset + off,
                   [[apx.ap[0][0], 128]] + dims_after_part)


def host_prep(inputs):
    W_in = np.asarray(inputs["L0_attn_W_in"])
    wA = np.concatenate([W_in[2048:4096], W_in[4160:4288], W_in[4288:4416],
                         W_in[4096:4160]], axis=0).T
    wAs = np.zeros((128, 18944), np.float32)
    for k in range(8):
        blk = wA[k * 128:(k + 1) * 128]
        for mc in range(16):
            wAs[:, (mc * 8 + k) * 128:(mc * 8 + k + 1) * 128] = \
                blk[:, mc * 128:(mc + 1) * 128]
        wAs[:, 16384 + k * 128:16384 + (k + 1) * 128] = blk[:, 2048:2176]
        wAs[:, 17408 + k * 128:17408 + (k + 1) * 128] = blk[:, 2176:2304]
        wAs[:, 18432 + k * 64:18432 + (k + 1) * 64] = blk[:, 2304:2368]

    def pack_km(wT, nk, nm):
        out = np.zeros((128, nk * nm * 128), np.float32)
        for k in range(nk):
            for mc in range(nm):
                out[:, (mc * nk + k) * 128:(mc * nk + k + 1) * 128] = \
                    wT[k * 128:(k + 1) * 128, mc * 128:(mc + 1) * 128]
        return out

    def pack_mk(wT, nk, nm):
        out = np.zeros((128, nk * nm * 128), np.float32)
        for k in range(nk):
            for mc in range(nm):
                out[:, (k * nm + mc) * 128:(k * nm + mc + 1) * 128] = \
                    wT[k * 128:(k + 1) * 128, mc * 128:(mc + 1) * 128]
        return out

    wZs = pack_km(W_in[0:2048].T.copy(), 8, 16)
    wRs = pack_km(np.asarray(inputs["L0_attn_W_res"]).T.copy(), 8, 16)
    wOs = pack_km(np.asarray(inputs["L0_attn_W_out"]).T.copy(), 16, 8)
    w1s = pack_km(np.asarray(inputs["L0_ffn_w1"]).copy(), 8, 32)
    w2s = pack_mk(np.asarray(inputs["L0_ffn_w2"]).copy(), 32, 8)
    wdts = np.asarray(inputs["L0_attn_W_dt"]).T.copy().astype(np.float32)

    convw = np.asarray(inputs["L0_attn_conv_w"])
    convw_c = convw.reshape(16, 128, 4).transpose(1, 0, 2).reshape(128, 64).copy()
    convb_c = np.asarray(inputs["L0_attn_conv_b"]).reshape(16, 128).T.copy()
    dskip_c = np.asarray(inputs["L0_attn_Dskip"]).reshape(2048).reshape(16, 128).T.copy()
    normw_c = np.asarray(inputs["L0_attn_norm_w"]).reshape(16, 128).T.copy()
    b1_c = np.asarray(inputs["L0_ffn_b1"]).reshape(32, 128).T.copy()
    b2_c = np.asarray(inputs["L0_ffn_b2"]).reshape(8, 128).T.copy()
    dtbias_c = np.asarray(inputs["L0_attn_dt_bias"]).reshape(8, 1).copy()
    alog_c = np.asarray(inputs["L0_attn_A_log"]).reshape(128, 1).copy()

    ones = np.ones((128, 128), np.float32)
    ident = np.eye(128, dtype=np.float32)
    e8 = np.zeros((8, 128), np.float32)
    for hh in range(8):
        e8[hh, hh * 16:(hh + 1) * 16] = 1.0
    hsel = np.zeros((128, 128), np.float32)
    for row in range(128):
        hsel[row, (row // 16)::8] = 1.0
    maskq = np.zeros((128, 128), np.uint8)
    for qs in range(128):
        hs, ss, cs = qs % 8, (qs // 8) % 4, qs // 32
        for qt in range(128):
            ht, st, ct = qt % 8, (qt // 8) % 4, qt // 32
            if hs == ht and cs == ct and ss <= st:
                maskq[qs, qt] = 1
    maskx = np.zeros((128, 96), np.uint8)
    for p in range(128):
        for c in range(96):
            if p % 8 == c % 8:
                maskx[p, c] = 1

    common = dict(wAs=wAs, wZs=wZs, wRs=wRs, wOs=wOs, w1s=w1s, w2s=w2s,
                  wdts=wdts, convw=convw_c.astype(np.float32),
                  convb=convb_c.astype(np.float32),
                  dskip=dskip_c.astype(np.float32),
                  normw=normw_c.astype(np.float32),
                  b1c=b1_c.astype(np.float32), b2c=b2_c.astype(np.float32),
                  dtbias=dtbias_c.astype(np.float32),
                  alog=alog_c.astype(np.float32),
                  ones=ones, ident=ident, e8=e8, hsel=hsel, maskq=maskq,
                  maskx=maskx, maskqf=maskq.astype(np.float32),
                  maskxf=maskx.astype(np.float32))
    return common


class MK:
    def __init__(self, L=4096, stages=("s0", "s1a", "s1b", "s2", "s3z", "s3r",
                                       "s3ga", "s3gb", "s4", "s5"),
                 dev=False):
        self.nc = bass.Bass("TRN2", target_bir_lowering=False)
        self.L = L
        self.NTT = L // TT
        self.NP = L // 16
        self.NG = L // 128
        self.FB = min(512, L)
        self.NFB = L // self.FB
        self.stages = set(stages)
        self.dev = dev
        self.es = ExitStack()
        self.sems = {}
        self.cnt = {}
        self.chain = {}
        self.bind = {}
        self.pcnt = {}
        self.freelist = {"hwd": [], "swd": [], "eng": [], "qsd": []}
        self._declare_dram()

    GLOBAL_SEMS = ("vch", "sch", "gch", "ldc")

    def _sem_class(self, name):
        if name.startswith("qs"):
            return "qsd"
        if name.startswith("ld"):
            return "hwd"
        if name.startswith("st"):
            return "swd"
        return "eng"

    def sem(self, name):
        if name not in self.bind:
            cls = self._sem_class(name)
            if self.freelist[cls]:
                h = self.freelist[cls].pop()
            else:
                h = self.es.enter_context(self.nc.semaphore())
                self.pcnt[h] = 0
            self.bind[name] = (h, self.pcnt[h])
            self.cnt[name] = 0
        return self.bind[name][0]

    def inc(self, bi, name, n=1):
        h = self.sem(name)
        bi.then_inc(h, n)
        self.pcnt[h] += n
        self.cnt[name] += n
        return self.cnt[name]

    def W(self, eng, name, v):
        h = self.sem(name)
        eng.wait_ge(h, self.bind[name][1] + v)

    def end_stage(self):
        self.chain = {k: v for k, v in self.chain.items()
                      if v[0] in self.GLOBAL_SEMS}
        self.nc.all_engine_barrier()
        for name in list(self.bind):
            if name in self.GLOBAL_SEMS:
                continue
            h, _ = self.bind.pop(name)
            self.freelist[self._sem_class(name)].append(h)

    def EV(self, eng, key, fn, *args, inc=None, n=1, **kw):
        prev = self.chain.get(key)
        if prev is not None and prev[0] in self.bind:
            self.W(eng, prev[0], prev[1])
        instr = getattr(eng, fn)(*args, **kw)
        nm = inc if inc is not None else (key + "ch")
        c = self.inc(instr, nm, n)
        self.chain[key] = (nm, c)
        return instr

    def _declare_dram(self):
        nc, L = self.nc, self.L
        def din(name, shape, dt=F32):
            return nc.dram_tensor(name, shape, dt, kind="ExternalInput")
        self.xT = din("xT", [D, L])
        self.wAs = din("wAs", [128, 18944])
        self.wZs = din("wZs", [128, 16384])
        self.wRs = din("wRs", [128, 16384])
        self.wOs = din("wOs", [128, 16384])
        self.w1s = din("w1s", [128, 32768])
        self.w2s = din("w2s", [128, 32768])
        self.wdts = din("wdts", [64, 8])
        self.convw = din("convw", [128, 64])
        self.convb = din("convb", [128, 16])
        self.dskip = din("dskip", [128, 16])
        self.normw = din("normw", [128, 16])
        self.b1c = din("b1c", [128, 32])
        self.b2c = din("b2c", [128, 8])
        self.dtbias = din("dtbias", [8, 1])
        self.alog = din("alog", [128, 1])
        self.ones = din("ones", [128, 128])
        self.ident = din("ident", [128, 128])
        self.e8 = din("e8", [8, 128])
        self.hsel = din("hsel", [128, 128])
        self.maskq = din("maskq", [128, 128], U8)
        self.maskx = din("maskx", [128, 96], U8)
        self.maskqf = din("maskqf", [128, 128])
        self.maskxf = din("maskxf", [128, 96])
        self.outT = nc.dram_tensor("outT", [D, L], F32, kind="ExternalOutput")

        kind = "ExternalOutput" if self.dev else "Internal"
        def scr(name, shape):
            return self.nc.dram_tensor(name, shape, F32, kind=kind)
        self.ln1h_d = scr("ln1h_d", [D, L])
        self.ln1l_d = scr("ln1l_d", [D, L])
        self.u_d = scr("u_d", [DI, L])
        self.U_d = scr("U_d", [L * 8, PH])
        self.y_d = scr("y_d", [L * 8, PH])
        self.sz_d = scr("sz_d", [DI, L])
        self.r_d = scr("r_d", [DI, L])
        self.g_d = scr("g_d", [DI, L])
        self.x2_d = scr("x2_d", [D, L])
        self.ln2_d = scr("ln2_d", [D, L])
        self.ln2h_d = scr("ln2h_d", [D, L])
        self.ln2l_d = scr("ln2l_d", [D, L])
        self.hgh_d = scr("hgh_d", [4096, L])
        self.hgl_d = scr("hgl_d", [4096, L])
        self.Bc_d = scr("Bc_d", [128, L])
        self.Cc_d = scr("Cc_d", [128, L])
        self.dtr_d = scr("dtr_d", [128, L])
        if self.dev:
            self.dbgB = scr("dbgB", [128, L])
            self.dbgC = scr("dbgC", [128, L])
            self.dbgdt = scr("dbgdt", [8, L])

    def _alloc_persistent(self):
        nc = self.nc
        a = nc.alloc_sbuf_tensor
        self.ones_s = a("ones_s", [128, 128], F32)
        self.ident_s = a("ident_s", [128, 128], F32)
        self.convw_s = a("convw_s", [128, 64], F32)
        self.convb_s = a("convb_s", [128, 16], F32)
        self.dskip_s = a("dskip_s", [128, 16], F32)
        self.normw_s = a("normw_s", [128, 16], F32)
        self.b1_s = a("b1_s", [128, 32], F32)
        self.b2_s = a("b2_s", [128, 8], F32)
        self.dtbias_s = a("dtbias_s", [8, 1], F32)
        self.alog_s = a("alog_s", [128, 1], F32)
        self.A_s = a("A_s", [128, 1], F32)
        self.Aexp_s = a("Aexp_s", [128, 1], F32)
        self.e8_s = a("e8_s", [8, 128], F32)
        self.hsel_s = a("hsel_s", [128, 128], F32)
        self.maskq_s = a("maskq_s", [128, 128], U8)
        self.maskx_s = a("maskx_s", [128, 96], U8)
        self.zero128 = a("zero128", [128, 128], F32)
        self.c_one = a("c_one", [128, 1], F32)
        self.c_epsln = a("c_epsln", [128, 1], F32)
        self.c_epsrms = a("c_epsrms", [128, 1], F32)
        self.wdt_s = a("wdt_s", [64, 8], F32)


    def s0(self):
        nc = self.nc
        self._alloc_persistent()
        with nc.Block("s0") as blk:
            @blk.sync
            def _(sync):
                for dst, srcd in [(self.ones_s, self.ones), (self.ident_s, self.ident),
                                  (self.convw_s, self.convw), (self.convb_s, self.convb),
                                  (self.dskip_s, self.dskip), (self.normw_s, self.normw),
                                  (self.b1_s, self.b1c), (self.b2_s, self.b2c),
                                  (self.dtbias_s, self.dtbias), (self.alog_s, self.alog),
                                  (self.e8_s, self.e8), (self.hsel_s, self.hsel),
                                  (self.maskq_s, self.maskq), (self.maskx_s, self.maskx),
                                  (self.wdt_s, self.wdts)]:
                    self.inc(sync.dma_start(out=dst.ap(), in_=srcd.ap()), "ldc", 16)

            n_ldc = self.cnt["ldc"]

            @blk.vector
            def _(vector):
                self.EV(vector, "v", "memset", self.zero128.ap(), 0.0)
                self.EV(vector, "v", "memset", self.c_one.ap(), 1.0)
                self.EV(vector, "v", "memset", self.c_epsln.ap(), EPS_LN)
                self.EV(vector, "v", "memset", self.c_epsrms.ap(), EPS_RMS)
                self.W(vector, "seA", 1)
                self.EV(vector, "v", "tensor_scalar_mul",
                        self.A_s.ap(), self.Aexp_s.ap(), -1.0, inc="vA")

            @blk.scalar
            def _(scalar):
                self.W(scalar, "ldc", n_ldc)
                self.EV(scalar, "s", "activation", out=self.Aexp_s.ap(),
                        in_=self.alog_s.ap(), func=AF.Exp, inc="seA")
        self.end_stage()

    def s1a(self):
        nc, L, NTT = self.nc, self.L, self.NTT
        n_ldc = self.cnt["ldc"]
        with ExitStack() as st:
            def sb(name, shape, dt=F32):
                return st.enter_context(nc.sbuf_tensor(name, shape, dt)).ap()
            wu = sb("wu", [128, 16384])
            xin = [sb(f"xin{j}", [128, 4096]) for j in range(2)]
            xsq = [sb(f"xsq{j}", [128, TT]) for j in range(2)]
            ln1b = sb("ln1b", [128, 4096])
            ln1h = sb("ln1h", [128, 4096], F32R)
            ln1l = sb("ln1l", [128, 4096], F32R)
            ub = [sb(f"ub{j}", [128, TT]) for j in range(4)]
            u_tm = [sb(f"utm{j}", [128, 2048]) for j in range(4)]
            halo = sb("halo", [128, 48])
            mu = sb("mu", [128, TT]); musq = sb("musq", [128, TT])
            vbuf = sb("vbuf", [128, TT]); vtmp = sb("vtmp", [128, TT])
            rstd = sb("rstd", [128, TT]); murstd = sb("murstd", [128, TT])
            tmp5 = sb("tmp5", [128, TT])
            pSA = st.enter_context(nc.psum_tensor("pSA", [128, 512], F32)).ap()
            pSB = st.enter_context(nc.psum_tensor("pSB", [128, 512], F32)).ap()
            pP = [st.enter_context(nc.psum_tensor(f"pP{j}", [128, 512], F32)).ap()
                  for j in range(2)]
            pT = [st.enter_context(nc.psum_tensor(f"pT{j}", [128, 512], F32)).ap()
                  for j in range(2)]
            pE = st.enter_context(nc.psum_tensor("pE1", [128, 512], F32)).ap()

            with nc.Block("s1a") as blk:
                @blk.sync
                def _(sync):
                    self.inc(sync.dma_start(out=wu, in_=self.wAs.ap()[:, 0:16384]),
                             "ldw1a", 16)
                    for i in range(NTT):
                        if i >= 2:
                            self.W(sync, "xfree", i - 1)
                            self.W(sync, "teA", i - 1)
                            self.W(sync, "sq", 8 * (i - 1))
                        srcx = _ap(self.xT, i * TT, [[L, 128], [128 * L, 8], [1, TT]])
                        self.inc(sync.dma_start(out=xin[i % 2], in_=srcx),
                                 f"ldx{i % 2}", 16)

                @blk.vector
                def _(vector):
                    self.EV(vector, "v", "memset", halo, 0.0)
                    for i in range(NTT):
                        self.W(vector, "teA", i + 1)
                        self.EV(vector, "v", "tensor_scalar_mul", mu, pSA[:, 0:TT],
                                1.0 / D)
                        self.EV(vector, "v", "tensor_mul", musq, mu, mu)
                        self.W(vector, "teSq", 8 * (i + 1))
                        if i > 0:
                            self.W(vector, "rstdS", i)
                        self.EV(vector, "v", "scalar_tensor_tensor", inc="vstat",
                                out=vbuf, in0=pSB[:, 0:TT], scalar=1.0 / D,
                                in1=musq, op0=AL.mult, op1=AL.subtract)
                        self.W(vector, "rstdS", i + 1)
                        if i > 0:
                            self.W(vector, "teP", i * 16)
                        self.EV(vector, "v", "tensor_mul", murstd, mu, rstd)
                        for k in range(8):
                            ksl = xin[i % 2][:, k * TT:(k + 1) * TT]
                            self.EV(vector, "v", "tensor_mul", tmp5, ksl, rstd,
                                    inc=("xfree" if k == 7 else None))
                            self.EV(vector, "v", "tensor_sub",
                                    ln1b[:, k * TT:(k + 1) * TT], tmp5, murstd,
                                    inc="ln1r")
                        if i > 0:
                            self.W(vector, "stLn", 32 * i)
                        for k in range(8):
                            bsl = ln1b[:, k * TT:(k + 1) * TT]
                            self.EV(vector, "v", "scalar_tensor_tensor",
                                    out=pE[:, 0:TT], in0=bsl, scalar=4097.0,
                                    in1=bsl, op0=AL.mult, op1=AL.subtract)
                            self.EV(vector, "v", "scalar_tensor_tensor",
                                    out=ln1h[:, k * TT:(k + 1) * TT], in0=bsl,
                                    scalar=4097.0, in1=pE[:, 0:TT],
                                    op0=AL.mult, op1=AL.subtract)
                            self.EV(vector, "v", "tensor_sub",
                                    ln1l[:, k * TT:(k + 1) * TT], bsl,
                                    ln1h[:, k * TT:(k + 1) * TT],
                                    inc=("ln1s" if k == 7 else None))
                        for fc in range(16):
                            self.W(vector, "teP", i * 16 + fc + 1)
                            j = i * 16 + fc
                            if j >= 4:
                                self.W(vector, "teT", (j - 4) * 4 + 4)
                                self.W(vector, f"stU{fc % 4}", 16 * (i * 4 + fc // 4))
                            ps = pP[fc % 2]
                            dst = ub[fc % 4]
                            wcol = lambda k: self.convw_s.ap()[:, fc * 4 + k:fc * 4 + k + 1]
                            bcol = self.convb_s.ap()[:, fc:fc + 1]
                            self.EV(vector, "v", "tensor_scalar", out=dst,
                                    in0=ps[:, 0:512], scalar1=wcol(3), scalar2=bcol,
                                    op0=AL.mult, op1=AL.add)
                            for k in range(3):
                                off = 3 - k
                                self.EV(vector, "v", "scalar_tensor_tensor",
                                        out=dst[:, off:512], in0=ps[:, 0:512 - off],
                                        scalar=wcol(k), in1=dst[:, off:512],
                                        op0=AL.mult, op1=AL.add)
                                self.EV(vector, "v", "scalar_tensor_tensor",
                                        out=dst[:, 0:off],
                                        in0=halo[:, fc * 3 + k:fc * 3 + 3],
                                        scalar=wcol(k), in1=dst[:, 0:off],
                                        op0=AL.mult, op1=AL.add)
                            self.EV(vector, "v", "tensor_copy",
                                    out=halo[:, fc * 3:fc * 3 + 3],
                                    in_=ps[:, 509:512], inc="cv")

                @blk.scalar
                def _(scalar):
                    def squares(i):
                        self.W(scalar, f"ldx{i % 2}", 16 * (i // 2 + 1))
                        for k in range(8):
                            if i * 8 + k >= 2:
                                self.W(scalar, "teSq", i * 8 + k - 1)
                            self.EV(scalar, "s", "activation", out=xsq[k % 2],
                                    in_=xin[i % 2][:, k * TT:(k + 1) * TT],
                                    func=AF.Square, inc="sq")
                    def rstds(i):
                        self.W(scalar, "vstat", i + 1)
                        if i > 0:
                            self.W(scalar, "xfree", i)
                        self.EV(scalar, "s", "activation", out=vtmp, in_=vbuf,
                                func=AF.Ln, bias=self.c_epsln.ap())
                        self.EV(scalar, "s", "activation", out=rstd, in_=vtmp,
                                func=AF.Exp, scale=-0.5, inc="rstdS")
                    squares(0)
                    rstds(0)
                    for i in range(NTT):
                        if i + 1 < NTT:
                            squares(i + 1)
                        for fc in range(16):
                            for g in range(4):
                                idx = i * 64 + fc * 4 + g
                                self.W(scalar, "teT", idx + 1)
                                if i > 0 and fc == 0 and g == 0:
                                    self.W(scalar, "stUM", 16 * i * 4)
                                self.EV(scalar, "s", "copy",
                                        out=u_tm[g][:, fc * 128:(fc + 1) * 128],
                                        in_=pT[idx % 2][:, 0:128], inc="evT")
                        if i + 1 < NTT:
                            rstds(i + 1)

                @blk.tensor
                def _(tensor):
                    self.W(tensor, "ldw1a", 16)
                    self.W(tensor, "ldc", n_ldc)
                    def stats(i):
                        self.W(tensor, f"ldx{i % 2}", 16 * (i // 2 + 1))
                        if i > 0:
                            self.W(tensor, "vstat", i)
                        for k in range(8):
                            it = tensor.matmul(pSA[:, 0:TT], self.ones_s.ap(),
                                               xin[i % 2][:, k * TT:(k + 1) * TT],
                                               start=(k == 0), stop=(k == 7))
                        self.inc(it, "teA", 1)
                        for k in range(8):
                            self.W(tensor, "sq", i * 8 + k + 1)
                            it = tensor.matmul(pSB[:, 0:TT], self.ones_s.ap(),
                                               xsq[k % 2], start=(k == 0),
                                               stop=(k == 7))
                            self.inc(it, "teSq", 1)
                    def body(i):
                        self.W(tensor, "ln1r", 8 * (i + 1))
                        def transposes(fc):
                            self.W(tensor, "cv", i * 16 + fc + 1)
                            for g in range(4):
                                idx = i * 64 + fc * 4 + g
                                if idx >= 2:
                                    self.W(tensor, "evT", idx - 1)
                                it2 = tensor.transpose(pT[idx % 2][:, 0:128],
                                                       ub[fc % 4][:, g * 128:(g + 1) * 128],
                                                       self.ident_s.ap())
                                self.inc(it2, "teT", 1)
                        for mc in range(16):
                            if i * 16 + mc >= 2:
                                self.W(tensor, "cv", i * 16 + mc - 1)
                            for k in range(8):
                                it = tensor.matmul(
                                    pP[mc % 2][:, 0:TT],
                                    wu[:, (mc * 8 + k) * 128:(mc * 8 + k + 1) * 128],
                                    ln1b[:, k * TT:(k + 1) * TT],
                                    start=(k == 0), stop=(k == 7))
                            self.inc(it, "teP", 1)
                            if mc >= 2:
                                transposes(mc - 2)
                        transposes(14)
                        transposes(15)
                    stats(0)
                    for i in range(NTT):
                        if i + 1 < NTT:
                            stats(i + 1)
                        body(i)

                @blk.gpsimd
                def _(gpsimd):
                    for i in range(NTT):
                        self.W(gpsimd, "ln1s", i + 1)
                        dst = _ap(self.ln1h_d, i * TT,
                                  [[L, 128], [128 * L, 8], [1, TT]]).bitcast(F32R)
                        self.inc(gpsimd.dma_start(out=dst, in_=ln1h), "stLn", 16)
                        dst = _ap(self.ln1l_d, i * TT,
                                  [[L, 128], [128 * L, 8], [1, TT]]).bitcast(F32R)
                        self.inc(gpsimd.dma_start(out=dst, in_=ln1l), "stLn", 16)
                        for fc in range(16):
                            self.W(gpsimd, "cv", i * 16 + fc + 1)
                            dst = _ap(self.u_d, fc * 128 * L + i * TT,
                                      [[L, 128], [1, TT]])
                            self.inc(gpsimd.dma_start(out=dst, in_=ub[fc % 4]),
                                     f"stU{fc % 4}", 16)
                        for g in range(4):
                            self.W(gpsimd, "evT", i * 64 + 15 * 4 + g + 1)
                            tok0 = i * TT + g * 128
                            dst = _ap(self.U_d, tok0 * 8 * PH, [[2048, 128], [1, 2048]])
                            self.inc(gpsimd.dma_start(out=dst, in_=u_tm[g]), "stUM", 16)
                    self.W(gpsimd, "stUM", self.cnt["stUM"])
                    for s4 in range(4):
                        if self.cnt.get(f"stU{s4}", 0):
                            self.W(gpsimd, f"stU{s4}", self.cnt[f"stU{s4}"])
                    self.W(gpsimd, "stLn", self.cnt["stLn"])
        self.end_stage()

    def s1b(self):
        nc, L, NTT = self.nc, self.L, self.NTT
        with ExitStack() as st:
            def sb(name, shape, dt=F32):
                return st.enter_context(nc.sbuf_tensor(name, shape, dt)).ap()
            wbc = sb("wbc", [128, 2560])
            wbce = sb("wbce", [128, 2560])
            wbch = sb("wbch", [128, 2560], F32R)
            wbcl = sb("wbcl", [128, 2560], F32R)
            linh = [sb(f"linh{j}", [128, 4096], F32R) for j in range(2)]
            linl = [sb(f"linl{j}", [128, 4096], F32R) for j in range(2)]
            dthb = sb("dthb", [64, TT])
            tmp8 = sb("tmp8", [8, TT])
            dtb = sb("dtb1", [8, TT])
            Bbuf = [sb(f"Bbuf{j}", [128, TT]) for j in range(2)]
            Cbuf = [sb(f"Cbuf{j}", [128, TT]) for j in range(2)]
            drb = [sb(f"drb{j}", [128, TT]) for j in range(2)]
            pB = st.enter_context(nc.psum_tensor("pB1", [128, 512], F32)).ap()
            pC = st.enter_context(nc.psum_tensor("pC1", [128, 512], F32)).ap()
            pDH = st.enter_context(nc.psum_tensor("pDH1", [128, 512], F32)).ap()
            pDT = st.enter_context(nc.psum_tensor("pDT1", [128, 512], F32)).ap()
            pRE = st.enter_context(nc.psum_tensor("pRE1", [128, 512], F32)).ap()

            with nc.Block("s1b") as blk:
                @blk.sync
                def _(sync):
                    self.inc(sync.dma_start(out=wbc, in_=self.wAs.ap()[:, 16384:18944]),
                             "ldw1b", 16)
                    for i in range(NTT):
                        if i >= 2:
                            self.W(sync, "teDH1", i - 1)
                        srcl = _ap(self.ln1h_d, i * TT,
                                   [[L, 128], [128 * L, 8], [1, TT]]).bitcast(F32R)
                        self.inc(sync.dma_start(out=linh[i % 2], in_=srcl),
                                 f"ldlh{i % 2}", 16)
                        srcl = _ap(self.ln1l_d, i * TT,
                                   [[L, 128], [128 * L, 8], [1, TT]]).bitcast(F32R)
                        self.inc(sync.dma_start(out=linl[i % 2], in_=srcl),
                                 f"ldll{i % 2}", 16)

                @blk.vector
                def _(vector):
                    self.W(vector, "ldw1b", 16)
                    self.EV(vector, "v", "scalar_tensor_tensor", out=wbce,
                            in0=wbc, scalar=4097.0, in1=wbc,
                            op0=AL.mult, op1=AL.subtract)
                    self.EV(vector, "v", "scalar_tensor_tensor", out=wbch,
                            in0=wbc, scalar=4097.0, in1=wbce,
                            op0=AL.mult, op1=AL.subtract)
                    self.EV(vector, "v", "tensor_sub", wbcl, wbc, wbch,
                            inc="wsp1b")
                    for i in range(NTT):
                        self.W(vector, "spS", i + 1)
                        if i >= 1:
                            self.W(vector, "teRE", i)
                        self.EV(vector, "v", "tensor_scalar",
                                out=dtb, in0=tmp8,
                                scalar1=1e-4, scalar2=1.0, op0=AL.max, op1=AL.min,
                                inc="dtr")

                @blk.tensor
                def _(tensor):
                    self.W(tensor, "wsp1b", 1)
                    def mm3(psum, coff, cw, i, name):
                        for k in range(8):
                            it = tensor.matmul(
                                psum, _ap(wbch.tensor, coff + k * cw,
                                          [[2560, 128], [1, cw]]),
                                linh[i % 2][:, k * TT:(k + 1) * TT],
                                start=(k == 0), stop=False)
                            it = tensor.matmul(
                                psum, _ap(wbch.tensor, coff + k * cw,
                                          [[2560, 128], [1, cw]]),
                                linl[i % 2][:, k * TT:(k + 1) * TT],
                                start=False, stop=False)
                            it = tensor.matmul(
                                psum, _ap(wbcl.tensor, coff + k * cw,
                                          [[2560, 128], [1, cw]]),
                                linh[i % 2][:, k * TT:(k + 1) * TT],
                                start=False, stop=(k == 7))
                        self.inc(it, name, 1)
                    for i in range(NTT):
                        self.W(tensor, f"ldlh{i % 2}", 16 * (i // 2 + 1))
                        self.W(tensor, f"ldll{i % 2}", 16 * (i // 2 + 1))
                        if i >= 1:
                            self.W(tensor, "evB", i)
                            self.W(tensor, "evC", i)
                            self.W(tensor, "evDTH", i)
                        mm3(pB[:, 0:TT], 0, 128, i, "teB1")
                        mm3(pC[:, 0:TT], 1024, 128, i, "teC1")
                        mm3(pDH[0:64, 0:TT], 2048, 64, i, "teDH1")
                        self.W(tensor, "evDTH", i + 1)
                        if i >= 1:
                            self.W(tensor, "spS", i)
                        it = tensor.matmul(pDT[0:8, 0:TT], self.wdt_s.ap(), dthb,
                                           start=True, stop=True)
                        self.inc(it, "teDT", 1)
                        self.W(tensor, "dtr", i + 1)
                        if i >= 1:
                            self.W(tensor, "evRE", i)
                        it = tensor.matmul(pRE[:, 0:TT], self.e8_s.ap(), dtb,
                                           start=True, stop=True)
                        self.inc(it, "teRE", 1)

                @blk.scalar
                def _(scalar):
                    for i in range(NTT):
                        self.W(scalar, "teB1", i + 1)
                        if i >= 2:
                            self.W(scalar, f"stB{i % 2}", 16 * (i // 2))
                        self.EV(scalar, "s", "copy", out=Bbuf[i % 2],
                                in_=pB[:, 0:TT], inc="evB")
                        self.W(scalar, "teC1", i + 1)
                        if i >= 2:
                            self.W(scalar, f"stC{i % 2}", 16 * (i // 2))
                        self.EV(scalar, "s", "copy", out=Cbuf[i % 2],
                                in_=pC[:, 0:TT], inc="evC")
                        self.W(scalar, "teDH1", i + 1)
                        if i >= 1:
                            self.W(scalar, "teDT", i)
                        self.EV(scalar, "s", "copy", out=dthb, in_=pDH[0:64, 0:TT],
                                inc="evDTH")
                        self.W(scalar, "teDT", i + 1)
                        if i >= 1:
                            self.W(scalar, "dtr", i)
                        self.EV(scalar, "s", "activation", out=tmp8,
                                in_=pDT[0:8, 0:TT], func=AF.Exp,
                                bias=self.dtbias_s.ap())
                        self.EV(scalar, "s", "activation", out=tmp8, in_=tmp8,
                                func=AF.Ln, bias=self.c_one.ap()[0:8, :], inc="spS")
                        self.W(scalar, "teRE", i + 1)
                        if i >= 2:
                            self.W(scalar, f"stDR{i % 2}", 16 * (i // 2))
                        self.EV(scalar, "s", "copy", out=drb[i % 2],
                                in_=pRE[:, 0:TT], inc="evRE")

                @blk.gpsimd
                def _(gpsimd):
                    for i in range(NTT):
                        self.W(gpsimd, "evB", i + 1)
                        self.inc(gpsimd.dma_start(
                            out=self.Bc_d.ap()[:, i * TT:(i + 1) * TT],
                            in_=Bbuf[i % 2]), f"stB{i % 2}", 16)
                        self.W(gpsimd, "evC", i + 1)
                        self.inc(gpsimd.dma_start(
                            out=self.Cc_d.ap()[:, i * TT:(i + 1) * TT],
                            in_=Cbuf[i % 2]), f"stC{i % 2}", 16)
                        self.W(gpsimd, "evRE", i + 1)
                        self.inc(gpsimd.dma_start(
                            out=self.dtr_d.ap()[:, i * TT:(i + 1) * TT],
                            in_=drb[i % 2]), f"stDR{i % 2}", 16)
                    for j in range(2):
                        for nm in (f"stB{j}", f"stC{j}", f"stDR{j}"):
                            if self.cnt.get(nm, 0):
                                self.W(gpsimd, nm, self.cnt[nm])
        self.end_stage()

    def s2(self):
        nc, L, NTT = self.nc, self.L, self.NTT
        NP = self.NP
        FB = self.FB
        NFB = self.NFB
        NPB = FB // 16
        NHG = NP // 8
        NHGF = NPB // 8
        with ExitStack() as st:
            def sb(name, shape, dt=F32):
                return st.enter_context(nc.sbuf_tensor(name, shape, dt)).ap()
            dt_rep = sb("dt_rep", [128, L])
            mkqf = sb("mkqf", [128, 128])
            mkxf = sb("mkxf", [128, 96])
            Bb = sb("Bb2", [128, L])
            Cb = sb("Cb2", [128, L])
            SCR = sb("SCR", [128, L])
            sB = sb("sBc", [128, L])
            nslice = [0]
            def argbuf(name):
                i = nslice[0]
                nslice[0] += 1
                if (i + 1) * FB <= L:
                    return SCR[:, i * FB:(i + 1) * FB]
                return sb(name, [128, FB])
            aAs = argbuf("aAs"); aASr = argbuf("aASr"); aC = argbuf("aC")
            aD = argbuf("aD")
            axc = [argbuf(f"axc{j}") for j in range(3)]
            axb = [argbuf(f"axb{j}") for j in range(3)]
            tB1 = sb("tB1", [128, FB]); tB2 = sb("tB2", [128, FB])
            tB3 = sb("tB3", [128, FB])
            Cd = [sb(f"Cd{j}", [128, FB]) for j in range(2)]
            CX = [[sb(f"CXf{j}_{k}", [128, FB]) for k in range(2)] for j in range(3)]
            Bd = sb("Bdf", [128, FB]); BD = sb("BDf", [128, FB])
            CS = sb("CSf", [128, FB])
            BX = [sb(f"BXf{j}", [128, FB]) for j in range(3)]
            se = [sb(f"se{j}", [128, FB]) for j in range(2)]
            lam = [sb(f"lam{j}", [128, NPB]) for j in range(2)]
            BDIAG = [sb(f"BDIAG{j}", [128, 1024]) for j in range(2)]
            BDbd = [sb(f"BDbd{j}", [128, 1024]) for j in range(2)]
            CSbd = [sb(f"CSbd{j}", [128, 1024]) for j in range(2)]
            BXbd = [[sb(f"BXbd{j}_{k}", [128, 256]) for k in range(2)]
                    for j in range(3)]
            U_s = [sb(f"Us{j}", [128, 256]) for j in range(4)]
            Mm = [sb(f"Mm{j}", [128, 128]) for j in range(2)]
            BdT = [sb(f"BdT{j}", [128, 128]) for j in range(2)]
            ygr = [sb(f"ygr{j}", [128, 1024]) for j in range(2)]
            h_s = [sb(f"hs{j}", [128, 256]) for j in range(2)]
            pM = [st.enter_context(nc.psum_tensor(f"pM{j}", [128, 512], F32)).ap()
                  for j in range(2)]
            pX = st.enter_context(nc.psum_tensor("pX", [128, 512], F32)).ap()
            pTt = st.enter_context(nc.psum_tensor("pTt0", [128, 512], F32)).ap()
            pY = [st.enter_context(nc.psum_tensor(f"pY{j}", [128, 512], F32)).ap()
                  for j in range(2)]
            pDH2 = [st.enter_context(nc.psum_tensor(f"pDH2{j}", [128, 512], F32)).ap()
                    for j in range(2)]

            A = self.A_s.ap()
            NPAIR = L // 16

            with nc.Block("s2") as blk:
                @blk.vector
                def _(vector):
                    self.EV(vector, "v", "memset", h_s[0], 0.0, inc="hrdy")
                    self.W(vector, "ldbc", 16 * 5)
                    def cstep(dst, srcb, d):
                        self.EV(vector, "v", "tensor_tensor",
                                out=_ap(dst.tensor, dst.offset + d,
                                        [[L, 128], [16, NPAIR], [1, 16 - d]]),
                                in0=_ap(srcb.tensor, srcb.offset + d,
                                        [[L, 128], [16, NPAIR], [1, 16 - d]]),
                                in1=_ap(srcb.tensor, srcb.offset,
                                        [[L, 128], [16, NPAIR], [1, 16 - d]]),
                                op=AL.add)
                        self.EV(vector, "v", "tensor_copy",
                                out=_ap(dst.tensor, dst.offset,
                                        [[L, 128], [16, NPAIR], [1, d]]),
                                in_=_ap(srcb.tensor, srcb.offset,
                                        [[L, 128], [16, NPAIR], [1, d]]))
                    cstep(SCR, dt_rep, 1)
                    cstep(sB, SCR, 2)
                    cstep(SCR, sB, 4)
                    cstep(sB, SCR, 8)

                    for fb in range(NFB):
                        base = fb * FB
                        scum = sB[:, base:base + FB]
                        dtr_sl = dt_rep[:, base:base + FB]
                        Bsl = Bb[:, base:base + FB]
                        Csl = Cb[:, base:base + FB]
                        if fb >= 1:
                            self.W(vector, "sxp", 11 * fb)
                            self.W(vector, "gbd", 2 * NHGF * fb)
                        if fb >= 2:
                            self.W(vector, "teM", NPB * (fb - 1))
                            self.W(vector, "teMx", NPB * (fb - 1))
                        self.EV(vector, "v", "tensor_scalar_mul", aAs, scum, A,
                                inc="vAs")
                        self.EV(vector, "v", "tensor_copy", out=tB1,
                                in_=_ap(sB.tensor, base + 15,
                                        [[L, 128], [16, NPB], [0, 16]]))
                        self.EV(vector, "v", "tensor_scalar_mul", aASr, tB1, A,
                                inc="vASr")
                        self.EV(vector, "v", "tensor_copy", out=tB1,
                                in_=_ap(sB.tensor, base + 3,
                                        [[L, 128], [16, NPB], [4, 4], [0, 4]]))
                        self.EV(vector, "v", "memset",
                                _ap(tB2.tensor, tB2.offset,
                                    [[FB, 128], [16, NPB], [1, 4]]), 0.0)
                        self.EV(vector, "v", "tensor_copy",
                                out=_ap(tB2.tensor, tB2.offset + 4,
                                        [[FB, 128], [16, NPB], [4, 3], [1, 4]]),
                                in_=_ap(sB.tensor, base + 3,
                                        [[L, 128], [16, NPB], [4, 3], [0, 4]]))
                        self.EV(vector, "v", "tensor_add", tB3, tB1, tB2)
                        self.EV(vector, "v", "tensor_scalar", out=tB1, in0=tB3,
                                scalar1=A, scalar2=0.5, op0=AL.mult, op1=AL.mult)
                        self.EV(vector, "v", "tensor_sub", aC, aAs, tB1,
                                inc="vargC")
                        for j in range(3):
                            self.EV(vector, "v", "tensor_copy", out=tB1,
                                    in_=_ap(sB.tensor, base + j * 4 + 3,
                                            [[L, 128], [16, NPB], [0, 16]]))
                            self.EV(vector, "v", "tensor_scalar_mul", tB2, tB1, A)
                            self.EV(vector, "v", "tensor_sub", tB3, aAs, tB2)
                            self.EV(vector, "v", "tensor_scalar_min", axc[j],
                                    tB3, 0.0, inc=f"vxc{j}")
                            self.EV(vector, "v", "tensor_scalar", out=axb[j],
                                    in0=tB3, scalar1=-1.0, scalar2=0.0,
                                    op0=AL.mult, op1=AL.min, inc=f"vxb{j}")
                        self.EV(vector, "v", "tensor_sub", aD, aASr, aAs,
                                inc="vargD")
                        self.W(vector, "sxp", 11 * fb + 3)
                        self.EV(vector, "v", "tensor_mul", Cd[fb % 2], Csl, se[1],
                                inc="vex")
                        self.W(vector, "sxp", 11 * fb + 2)
                        self.EV(vector, "v", "tensor_mul", tB1, se[0], dtr_sl)
                        self.EV(vector, "v", "tensor_mul", Bd, Bsl, tB1, inc="vex")
                        for j in range(3):
                            if j == 0 and fb >= 1:
                                self.W(vector, "vbd", 4 * NHGF * fb)
                            self.W(vector, "sxp", 11 * fb + 5 + 2 * j)
                            self.EV(vector, "v", "tensor_mul", CX[j][fb % 2],
                                    Csl, se[1], inc="vex")
                            self.W(vector, "sxp", 11 * fb + 4 + 2 * j)
                            self.EV(vector, "v", "tensor_mul", tB1, se[0], dtr_sl)
                            self.EV(vector, "v", "tensor_mul", BX[j], Bsl, tB1,
                                    inc="vex")
                        self.W(vector, "sxp", 11 * fb + 10)
                        self.EV(vector, "v", "tensor_mul", CS, Csl, se[1],
                                inc="vex")
                        self.W(vector, "sxp", 11 * fb + 11)
                        self.EV(vector, "v", "tensor_mul", tB1, se[0], dtr_sl)
                        self.EV(vector, "v", "tensor_mul", BD, Bsl, tB1, inc="vex")


                        def v_upd(k):
                            kf = k // NPB
                            kprf = k % NPB
                            self.W(vector, "teDH2", k + 1)
                            self.W(vector, "teY", k + 1)
                            if kprf == 0:
                                self.W(vector, "sxp", 11 * kf + 1)
                            self.EV(vector, "v", "scalar_tensor_tensor",
                                    out=h_s[(k + 1) % 2], in0=h_s[k % 2],
                                    scalar=lam[kf % 2][:, kprf:kprf + 1],
                                    in1=pDH2[k % 2][:, 0:256],
                                    op0=AL.mult, op1=AL.add, inc="hrdy")

                        hg0 = fb * NHGF
                        for hgl in range(NHGF):
                            hg = hg0 + hgl
                            for prl8 in range(8):
                                pr = hg * 8 + prl8
                                self.W(vector, "teM", pr + 1)
                                if pr >= 2:
                                    self.W(vector, "teY", pr - 1)
                                self.EV(vector, "v", "tensor_mul",
                                        Mm[pr % 2], pM[pr % 2][:, 0:128],
                                        mkqf)
                                self.W(vector, "teMx", pr + 1)
                                for j in range(3):
                                    w = 32 * (3 - j)
                                    self.EV(vector, "v", "tensor_mul",
                                            _ap(Mm[pr % 2].tensor,
                                                Mm[pr % 2].offset
                                                + 32 * j * 128 + 32 * (j + 1),
                                                [[128, 32], [1, w]]),
                                            pX[32 * j:32 * (j + 1), 0:w],
                                            mkxf[32 * j:32 * (j + 1), 0:w],
                                            inc=("selc" if j == 2 else None))
                                if pr >= 1:
                                    v_upd(pr - 1)
                        if fb == NFB - 1:
                            v_upd(NP - 1)

                @blk.scalar
                def _(scalar):
                    for fb in range(NFB):
                        self.W(scalar, "vASr", fb + 1)
                        if fb >= 2:
                            self.W(scalar, "hrdy", NPB * (fb - 1) + 1)
                        self.EV(scalar, "s", "activation", out=lam[fb % 2],
                                in_=_sv(aASr, 0, [[16, NPB]]),
                                func=AF.Exp, inc="sxp")
                        def se_exp(slot, reuse_vex, arg, scale=1.0):
                            if reuse_vex >= 1:
                                self.W(scalar, "vex", reuse_vex)
                            self.EV(scalar, "s", "activation", out=se[slot],
                                    in_=arg, func=AF.Exp, scale=scale, inc="sxp")
                        self.W(scalar, "vargC", fb + 1)
                        se_exp(0, 10 * fb, aC, -1.0)
                        se_exp(1, 10 * fb - 1, aC, 1.0)
                        for j in range(3):
                            self.W(scalar, f"vxb{j}", fb + 1)
                            se_exp(0, 10 * fb + 2 + 2 * j, axb[j])
                            self.W(scalar, f"vxc{j}", fb + 1)
                            se_exp(1, 10 * fb + 1 + 2 * j, axc[j])
                        self.W(scalar, "vAs", fb + 1)
                        se_exp(1, 10 * fb + 7, aAs)
                        self.W(scalar, "vargD", fb + 1)
                        se_exp(0, 10 * fb + 8, aD)
                        def evacT(k):
                            self.W(scalar, "teTr", k + 1)
                            if k >= 2:
                                self.W(scalar, "teDH2", k - 1)
                            self.EV(scalar, "s", "copy", out=BdT[k % 2],
                                    in_=pTt[:, 0:128], inc="evTr")
                        for prf in range(NPB):
                            pr = fb * NPB + prf
                            qg = pr // 4
                            if prf == 0:
                                evacT(pr)
                            if pr + 1 < NP and prf + 1 < NPB:
                                evacT(pr + 1)
                            self.W(scalar, "teY", pr + 1)
                            if pr % 4 == 0 and qg >= 2:
                                self.W(scalar, f"stY{qg % 2}", 16 * (qg // 2))
                            self.EV(scalar, "s", "copy",
                                    out=ygr[qg % 2][:, (pr % 4) * 256:(pr % 4) * 256 + 256],
                                    in_=pY[pr % 2][:, 0:256], inc="evY")

                @blk.tensor
                def _(tensor):
                    def dfront(pr):
                        fb = pr // NPB
                        prf = pr % NPB
                        hg = pr // 8
                        prl8 = pr % 8
                        slot = hg % 2
                        tb = prf * 16
                        if prf == 0:
                            self.W(tensor, "vex", 10 * (fb + 1))
                        self.W(tensor, "vbd", 4 * (hg + 1))
                        self.W(tensor, "gbd", 2 * (hg + 1))
                        if pr >= 2:
                            self.W(tensor, "selc", pr - 1)
                        it = tensor.matmul(
                            pM[pr % 2][:, 0:128],
                            BDIAG[slot][:, prl8 * 128:prl8 * 128 + 128],
                            _ap(Cd[fb % 2].tensor, Cd[fb % 2].offset + tb,
                                [[FB, 128], [1, 16], [0, 8]]),
                            start=True, stop=True)
                        self.inc(it, "teM", 1)
                        if pr >= 1:
                            self.W(tensor, "selc", pr)
                        for j in range(3):
                            w = 32 * (3 - j)
                            it = tensor.matmul(
                                pX[j * 32:(j + 1) * 32, 0:w],
                                BXbd[j][slot][:, prl8 * 32:prl8 * 32 + 32],
                                _ap(CX[j][fb % 2].tensor,
                                    CX[j][fb % 2].offset + tb + (j + 1) * 4,
                                    [[FB, 128], [1, 4 * (3 - j)], [0, 8]]),
                                start=True, stop=True)
                            if j == 2:
                                self.inc(it, "teMx", 1)
                        if pr >= 1:
                            self.W(tensor, "evTr", pr)
                        it = tensor.transpose(pTt[:, 0:128],
                                              BDbd[slot][:, prl8 * 128:prl8 * 128 + 128],
                                              self.ident_s.ap())
                        self.inc(it, "teTr", 1)

                    def yback(pr):
                        fb = pr // NPB
                        hg = pr // 8
                        prl8 = pr % 8
                        slot = hg % 2
                        self.W(tensor, f"ldU{pr % 4}", 16 * (pr // 4 + 1))
                        self.W(tensor, "selc", pr + 1)
                        self.W(tensor, "hrdy", pr + 1)
                        if pr >= 2:
                            self.W(tensor, "evY", pr - 1)
                        it = tensor.matmul(pY[pr % 2][:, 0:256], Mm[pr % 2],
                                           U_s[pr % 4], start=True, stop=False)
                        it = tensor.matmul(pY[pr % 2][:, 0:256],
                                           CSbd[slot][:, prl8 * 128:prl8 * 128 + 128],
                                           h_s[pr % 2], start=False, stop=True)
                        self.inc(it, "teY", 1)
                        self.W(tensor, "evTr", pr + 1)
                        if pr >= 2:
                            self.W(tensor, "hrdy", pr)
                        it = tensor.matmul(pDH2[pr % 2][:, 0:256], BdT[pr % 2],
                                           U_s[pr % 4], start=True, stop=True)
                        self.inc(it, "teDH2", 1)

                    for pr in range(NP + 1):
                        boundary = (pr % 8 == 0)
                        if pr >= 1 and boundary:
                            yback(pr - 1)
                        if pr < NP:
                            dfront(pr)
                        if pr >= 1 and not boundary:
                            yback(pr - 1)

                @blk.gpsimd
                def _(gpsimd):
                    def gbd_build(hg):
                        fb = (hg * 8) // NPB
                        hgl = hg - fb * NHGF
                        tb = hgl * 8 * 16
                        slot = hg % 2
                        if hg >= 2:
                            self.W(gpsimd, "teY", 8 * (hg - 1))
                        self.W(gpsimd, "vex", 10 * fb + 2)
                        self.EV(gpsimd, "g", "tensor_tensor",
                                out=_ap(BDIAG[slot].tensor, 0,
                                        [[1024, 128], [128, 8], [8, 16], [1, 8]]),
                                in0=_ap(Bd.tensor, Bd.offset + tb,
                                        [[FB, 128], [16, 8], [1, 16], [0, 8]]),
                                in1=_ap(self.hsel_s.ap().tensor, 0,
                                        [[128, 128], [0, 8], [8, 16], [1, 8]]),
                                op=AL.mult, inc="gbd")
                        self.W(gpsimd, "vex", 10 * fb + 10)
                        self.EV(gpsimd, "g", "tensor_tensor",
                                out=_ap(BDbd[slot].tensor, 0,
                                        [[1024, 128], [128, 8], [8, 16], [1, 8]]),
                                in0=_ap(BD.tensor, BD.offset + tb,
                                        [[FB, 128], [16, 8], [1, 16], [0, 8]]),
                                in1=_ap(self.hsel_s.ap().tensor, 0,
                                        [[128, 128], [0, 8], [8, 16], [1, 8]]),
                                op=AL.mult, inc="gbd")
                        self.EV(gpsimd, "g", "tensor_tensor",
                                out=_ap(CSbd[slot].tensor, 0,
                                        [[1024, 128], [128, 8], [8, 16], [1, 8]]),
                                in0=_ap(CS.tensor, CS.offset + tb,
                                        [[FB, 128], [16, 8], [1, 16], [0, 8]]),
                                in1=_ap(self.hsel_s.ap().tensor, 0,
                                        [[128, 128], [0, 8], [8, 16], [1, 8]]),
                                op=AL.mult, inc="vbd")
                        for j in range(3):
                            self.EV(gpsimd, "g", "tensor_tensor",
                                    out=_ap(BXbd[j][slot].tensor, 0,
                                            [[256, 128], [32, 8], [8, 4], [1, 8]]),
                                    in0=_ap(BX[j].tensor,
                                            BX[j].offset + tb + j * 4,
                                            [[FB, 128], [16, 8], [1, 4], [0, 8]]),
                                    in1=_ap(self.hsel_s.ap().tensor, 0,
                                            [[128, 128], [0, 8], [8, 4], [1, 8]]),
                                    op=AL.mult, inc="vbd")

                    for hg in range(NHG):
                        hgl = hg % NHGF
                        if hgl == 0:
                            gbd_build(hg)
                            if NHGF > 1 and hg + 1 < NHG:
                                gbd_build(hg + 1)
                        elif hgl + 1 < NHGF and hg + 1 < NHG:
                            gbd_build(hg + 1)
                        for qg in (2 * hg, 2 * hg + 1):
                            self.W(gpsimd, "evY", 4 * (qg + 1))
                            dst = _ap(self.y_d, qg * 4 * 128 * PH,
                                      [[256, 128], [128 * 256, 4], [1, 256]])
                            self.inc(gpsimd.dma_start(out=dst, in_=ygr[qg % 2]),
                                     f"stY{qg % 2}", 16)
                    for s2_ in range(2):
                        if self.cnt.get(f"stY{s2_}", 0):
                            self.W(gpsimd, f"stY{s2_}", self.cnt[f"stY{s2_}"])

                @blk.sync
                def _(sync):
                    self.inc(sync.dma_start(out=Bb, in_=self.Bc_d.ap()), "ldbc", 16)
                    self.inc(sync.dma_start(out=Cb, in_=self.Cc_d.ap()), "ldbc", 16)
                    self.inc(sync.dma_start(out=dt_rep, in_=self.dtr_d.ap()),
                             "ldbc", 16)
                    self.inc(sync.dma_start(out=mkqf, in_=self.maskqf.ap()),
                             "ldbc", 16)
                    self.inc(sync.dma_start(out=mkxf, in_=self.maskxf.ap()),
                             "ldbc", 16)
                    for pr in range(NP):
                        if pr >= 4:
                            self.W(sync, "teDH2", pr - 3)
                        src = _ap(self.U_d, pr * 128 * PH, [[256, 128], [1, 256]])
                        self.inc(sync.dma_start(out=U_s[pr % 4], in_=src),
                                 f"ldU{pr % 4}", 16)
        self.end_stage()

    def s3z(self):
        nc, L, NTT = self.nc, self.L, self.NTT
        with ExitStack() as st:
            def sb(name, shape, dt=F32):
                return st.enter_context(nc.sbuf_tensor(name, shape, dt)).ap()
            wst = [sb(f"wst_s3z{j}", [128, 512]) for j in range(2)]
            wh = sb("wh_s3z", [128, 16384], F32R)
            wl = sb("wl_s3z", [128, 16384], F32R)
            we = sb("we_s3z", [128, 512])
            linh = [sb(f"linhs3z" + str(j), [128, 4096], F32R) for j in range(2)]
            linl = [sb(f"linls3z" + str(j), [128, 4096], F32R) for j in range(2)]
            ob = [sb(f"obs3z{j}", [128, TT]) for j in range(2)]
            pZ = [st.enter_context(nc.psum_tensor(f"ps3z{j}", [128, 512], F32)).ap()
                  for j in range(2)]

            with nc.Block("s3z") as blk:
                @blk.sync
                def _(sync):
                    for c in range(32):
                        if c >= 2:
                            self.W(sync, "wsps3z", c - 1)
                        self.inc(sync.dma_start(
                            out=wst[c % 2],
                            in_=self.wZs.ap()[:, c * 512:(c + 1) * 512]),
                            f"ldws3z{c % 2}", 16)
                    for i in range(NTT):
                        if i >= 2:
                            self.W(sync, "tes3z", (i - 1) * 16)
                        srcl = _ap(self.ln1h_d, i * TT,
                                   [[L, 128], [128 * L, 8], [1, TT]]).bitcast(F32R)
                        self.inc(sync.dma_start(out=linh[i % 2], in_=srcl),
                                 f"ldhs3z" + str(i % 2), 16)
                        srcl = _ap(self.ln1l_d, i * TT,
                                   [[L, 128], [128 * L, 8], [1, TT]]).bitcast(F32R)
                        self.inc(sync.dma_start(out=linl[i % 2], in_=srcl),
                                 f"ldls3z" + str(i % 2), 16)

                @blk.vector
                def _(vector):
                    for c in range(32):
                        self.W(vector, f"ldws3z{c % 2}", 16 * (c // 2 + 1))
                        csl = wh[:, c * 512:(c + 1) * 512]
                        lsl = wl[:, c * 512:(c + 1) * 512]
                        self.EV(vector, "v", "scalar_tensor_tensor", out=we,
                                in0=wst[c % 2], scalar=4097.0, in1=wst[c % 2],
                                op0=AL.mult, op1=AL.subtract)
                        self.EV(vector, "v", "scalar_tensor_tensor", out=csl,
                                in0=wst[c % 2], scalar=4097.0, in1=we,
                                op0=AL.mult, op1=AL.subtract)
                        self.EV(vector, "v", "tensor_sub", lsl, wst[c % 2], csl,
                                inc="wsps3z")

                @blk.tensor
                def _(tensor):
                    for i in range(NTT):
                        self.W(tensor, f"ldhs3z" + str(i % 2), 16 * (i // 2 + 1))
                        self.W(tensor, f"ldls3z" + str(i % 2), 16 * (i // 2 + 1))
                        for mc in range(16):
                            idx = i * 16 + mc
                            if i == 0:
                                self.W(tensor, "wsps3z", (mc + 1) * 2)
                            if idx >= 2:
                                self.W(tensor, "evs3z", idx - 1)
                            for k in range(8):
                                co = (mc * 8 + k) * 128
                                it = tensor.matmul(
                                    pZ[mc % 2][:, 0:TT], wh[:, co:co + 128],
                                    linh[i % 2][:, k * TT:(k + 1) * TT],
                                    start=(k == 0), stop=False)
                                it = tensor.matmul(
                                    pZ[mc % 2][:, 0:TT], wh[:, co:co + 128],
                                    linl[i % 2][:, k * TT:(k + 1) * TT],
                                    start=False, stop=False)
                                it = tensor.matmul(
                                    pZ[mc % 2][:, 0:TT], wl[:, co:co + 128],
                                    linh[i % 2][:, k * TT:(k + 1) * TT],
                                    start=False, stop=(k == 7))
                            self.inc(it, "tes3z", 1)

                @blk.scalar
                def _(scalar):
                    for i in range(NTT):
                        for mc in range(16):
                            idx = i * 16 + mc
                            self.W(scalar, "tes3z", idx + 1)
                            if idx >= 2:
                                self.W(scalar, f"sts3z{mc % 2}", 16 * (idx // 2))
                            self.EV(scalar, "s", "activation", out=ob[mc % 2], in_=pZ[mc % 2][:, 0:TT], func=AF.Silu, inc="evs3z")

                @blk.gpsimd
                def _(gpsimd):
                    for i in range(NTT):
                        for mc in range(16):
                            idx = i * 16 + mc
                            self.W(gpsimd, "evs3z", idx + 1)
                            dst = _ap(self.sz_d, mc * 128 * L + i * TT,
                                      [[L, 128], [1, TT]])
                            self.inc(gpsimd.dma_start(out=dst, in_=ob[mc % 2]),
                                     f"sts3z{mc % 2}", 16)
                    for j in range(2):
                        nmm = f"sts3z{j}"
                        if self.cnt.get(nmm, 0):
                            self.W(gpsimd, nmm, self.cnt[nmm])
        self.end_stage()

    def s3r(self):
        nc, L, NTT = self.nc, self.L, self.NTT
        with ExitStack() as st:
            def sb(name, shape, dt=F32):
                return st.enter_context(nc.sbuf_tensor(name, shape, dt)).ap()
            wst = [sb(f"wst_s3r{j}", [128, 512]) for j in range(2)]
            wh = sb("wh_s3r", [128, 16384], F32R)
            wl = sb("wl_s3r", [128, 16384], F32R)
            we = sb("we_s3r", [128, 512])
            linh = [sb(f"linhs3r" + str(j), [128, 4096], F32R) for j in range(2)]
            linl = [sb(f"linls3r" + str(j), [128, 4096], F32R) for j in range(2)]
            ob = [sb(f"obs3r{j}", [128, TT]) for j in range(2)]
            pZ = [st.enter_context(nc.psum_tensor(f"ps3r{j}", [128, 512], F32)).ap()
                  for j in range(2)]

            with nc.Block("s3r") as blk:
                @blk.sync
                def _(sync):
                    for c in range(32):
                        if c >= 2:
                            self.W(sync, "wsps3r", c - 1)
                        self.inc(sync.dma_start(
                            out=wst[c % 2],
                            in_=self.wRs.ap()[:, c * 512:(c + 1) * 512]),
                            f"ldws3r{c % 2}", 16)
                    for i in range(NTT):
                        if i >= 2:
                            self.W(sync, "tes3r", (i - 1) * 16)
                        srcl = _ap(self.ln1h_d, i * TT,
                                   [[L, 128], [128 * L, 8], [1, TT]]).bitcast(F32R)
                        self.inc(sync.dma_start(out=linh[i % 2], in_=srcl),
                                 f"ldhs3r" + str(i % 2), 16)
                        srcl = _ap(self.ln1l_d, i * TT,
                                   [[L, 128], [128 * L, 8], [1, TT]]).bitcast(F32R)
                        self.inc(sync.dma_start(out=linl[i % 2], in_=srcl),
                                 f"ldls3r" + str(i % 2), 16)

                @blk.vector
                def _(vector):
                    for c in range(32):
                        self.W(vector, f"ldws3r{c % 2}", 16 * (c // 2 + 1))
                        csl = wh[:, c * 512:(c + 1) * 512]
                        lsl = wl[:, c * 512:(c + 1) * 512]
                        self.EV(vector, "v", "scalar_tensor_tensor", out=we,
                                in0=wst[c % 2], scalar=4097.0, in1=wst[c % 2],
                                op0=AL.mult, op1=AL.subtract)
                        self.EV(vector, "v", "scalar_tensor_tensor", out=csl,
                                in0=wst[c % 2], scalar=4097.0, in1=we,
                                op0=AL.mult, op1=AL.subtract)
                        self.EV(vector, "v", "tensor_sub", lsl, wst[c % 2], csl,
                                inc="wsps3r")

                @blk.tensor
                def _(tensor):
                    for i in range(NTT):
                        self.W(tensor, f"ldhs3r" + str(i % 2), 16 * (i // 2 + 1))
                        self.W(tensor, f"ldls3r" + str(i % 2), 16 * (i // 2 + 1))
                        for mc in range(16):
                            idx = i * 16 + mc
                            if i == 0:
                                self.W(tensor, "wsps3r", (mc + 1) * 2)
                            if idx >= 2:
                                self.W(tensor, "evs3r", idx - 1)
                            for k in range(8):
                                co = (mc * 8 + k) * 128
                                it = tensor.matmul(
                                    pZ[mc % 2][:, 0:TT], wh[:, co:co + 128],
                                    linh[i % 2][:, k * TT:(k + 1) * TT],
                                    start=(k == 0), stop=False)
                                it = tensor.matmul(
                                    pZ[mc % 2][:, 0:TT], wh[:, co:co + 128],
                                    linl[i % 2][:, k * TT:(k + 1) * TT],
                                    start=False, stop=False)
                                it = tensor.matmul(
                                    pZ[mc % 2][:, 0:TT], wl[:, co:co + 128],
                                    linh[i % 2][:, k * TT:(k + 1) * TT],
                                    start=False, stop=(k == 7))
                            self.inc(it, "tes3r", 1)

                @blk.scalar
                def _(scalar):
                    for i in range(NTT):
                        for mc in range(16):
                            idx = i * 16 + mc
                            self.W(scalar, "tes3r", idx + 1)
                            if idx >= 2:
                                self.W(scalar, f"sts3r{mc % 2}", 16 * (idx // 2))
                            self.EV(scalar, "s", "copy", out=ob[mc % 2], in_=pZ[mc % 2][:, 0:TT], inc="evs3r")

                @blk.gpsimd
                def _(gpsimd):
                    for i in range(NTT):
                        for mc in range(16):
                            idx = i * 16 + mc
                            self.W(gpsimd, "evs3r", idx + 1)
                            dst = _ap(self.r_d, mc * 128 * L + i * TT,
                                      [[L, 128], [1, TT]])
                            self.inc(gpsimd.dma_start(out=dst, in_=ob[mc % 2]),
                                     f"sts3r{mc % 2}", 16)
                    for j in range(2):
                        nmm = f"sts3r{j}"
                        if self.cnt.get(nmm, 0):
                            self.W(gpsimd, nmm, self.cnt[nmm])
        self.end_stage()

    def s3ga(self):
        nc, L = self.nc, self.L
        NG = self.NG
        with ExitStack() as st:
            def sb(name, shape, dt=F32):
                return st.enter_context(nc.sbuf_tensor(name, shape, dt)).ap()
            ytm = [sb(f"ytm{j}", [128, 2048]) for j in range(2)]
            ug = [sb(f"ug{j}", [128, 2048]) for j in range(2)]
            szg = [sb(f"szg{j}", [128, 2048]) for j in range(2)]
            rg = [sb(f"rg{j}", [128, 2048]) for j in range(2)]
            gbuf = [sb(f"gbuf{j}", [128, 2048]) for j in range(6)]
            gsq = [sb(f"gsq{j}", [128, 128]) for j in range(2)]
            rstd2b = [sb(f"rstd2b{j}", [128, 128]) for j in range(3)]
            mstA = sb("mstA", [128, 128])
            pYT = [st.enter_context(nc.psum_tensor(f"pYT{j}", [128, 512], F32)).ap()
                   for j in range(2)]
            pSg = st.enter_context(nc.psum_tensor("pSg", [128, 512], F32)).ap()

            with nc.Block("s3ga") as blk:
                @blk.sync
                def _(sync):
                    for g in range(NG):
                        if g >= 2:
                            self.W(sync, "teYT", 16 * (g - 1))
                            self.W(sync, "gat0", 16 * (g - 1))
                            self.W(sync, "gat1", g - 1)
                        self.inc(sync.dma_start(
                            out=ytm[g % 2],
                            in_=_ap(self.y_d, g * 128 * 2048, [[2048, 128], [1, 2048]])),
                            f"ldyA{g % 2}", 16)
                        self.inc(sync.dma_start(
                            out=ug[g % 2],
                            in_=_ap(self.u_d, g * 128, [[L, 128], [128 * L, 16], [1, 128]])),
                            f"lduA{g % 2}", 16)
                        self.inc(sync.dma_start(
                            out=szg[g % 2],
                            in_=_ap(self.sz_d, g * 128, [[L, 128], [128 * L, 16], [1, 128]])),
                            f"ldszA{g % 2}", 16)
                        self.inc(sync.dma_start(
                            out=rg[g % 2],
                            in_=_ap(self.r_d, g * 128, [[L, 128], [128 * L, 16], [1, 128]])),
                            f"ldrA{g % 2}", 16)

                @blk.tensor
                def _(tensor):
                    for g in range(NG):
                        self.W(tensor, f"ldyA{g % 2}", 16 * (g // 2 + 1))
                        for fc in range(16):
                            idx = g * 16 + fc
                            if idx >= 2:
                                self.W(tensor, "gat0", idx - 1)
                            it = tensor.transpose(
                                pYT[fc % 2][:, 0:128],
                                ytm[g % 2][:, fc * 128:(fc + 1) * 128],
                                self.ident_s.ap())
                            self.inc(it, "teYT", 1)
                        for fc in range(16):
                            idx = g * 16 + fc
                            self.W(tensor, "sqg", idx + 1)
                            if fc == 0 and g >= 1:
                                self.W(tensor, "rstd2S", g)
                            it = tensor.matmul(pSg[:, 0:128], self.ones_s.ap(),
                                               gsq[fc % 2], start=(fc == 0),
                                               stop=(fc == 15))
                            self.inc(it, "teSgF", 1)

                @blk.vector
                def _(vector):
                    def gnr(gg):
                        self.W(vector, "rstd2S", gg + 1)
                        for fc in range(16):
                            self.EV(vector, "v", "scalar_tensor_tensor",
                                    out=gbuf[gg % 6][:, fc * 128:(fc + 1) * 128],
                                    in0=gbuf[gg % 6][:, fc * 128:(fc + 1) * 128],
                                    scalar=self.normw_s.ap()[:, fc:fc + 1],
                                    in1=rstd2b[gg % 3], op0=AL.mult,
                                    op1=AL.mult, inc="gnr")
                    for g in range(NG):
                        nld = 16 * (g // 2 + 1)
                        self.W(vector, f"lduA{g % 2}", nld)
                        self.W(vector, f"ldszA{g % 2}", nld)
                        self.W(vector, f"ldrA{g % 2}", nld)
                        if g >= 6:
                            self.W(vector, f"stg{g % 6}", 16 * (g // 6))
                            self.W(vector, "sqg", 16 * (g - 5))
                        for fc in range(16):
                            self.W(vector, "teYT", g * 16 + fc + 1)
                            self.EV(vector, "v", "scalar_tensor_tensor",
                                    out=gbuf[g % 6][:, fc * 128:(fc + 1) * 128],
                                    in0=ug[g % 2][:, fc * 128:(fc + 1) * 128],
                                    scalar=self.dskip_s.ap()[:, fc:fc + 1],
                                    in1=pYT[fc % 2][:, 0:128],
                                    op0=AL.mult, op1=AL.add, inc="gat0")
                        self.EV(vector, "v", "tensor_mul", gbuf[g % 6],
                                gbuf[g % 6], szg[g % 2])
                        self.EV(vector, "v", "tensor_add", gbuf[g % 6],
                                gbuf[g % 6], rg[g % 2], inc="gat1")
                        if g >= 2:
                            gnr(g - 2)
                    for gg in range(NG - 2, NG):
                        gnr(gg)

                @blk.scalar
                def _(scalar):
                    for g in range(NG):
                        self.W(scalar, "gat1", g + 1)
                        for fc in range(16):
                            idx = g * 16 + fc
                            if idx >= 2:
                                self.W(scalar, "teSgF", idx - 1)
                            self.EV(scalar, "s", "activation", out=gsq[fc % 2],
                                    in_=gbuf[g % 6][:, fc * 128:(fc + 1) * 128],
                                    func=AF.Square, inc="sqg")
                        self.W(scalar, "teSgF", 16 * (g + 1))
                        if g >= 3:
                            self.W(scalar, "gnr", 16 * (g - 2))
                        self.EV(scalar, "s", "activation", out=mstA,
                                in_=pSg[:, 0:128], func=AF.Ln,
                                bias=self.c_epsrms.ap(), scale=1.0 / DI)
                        self.EV(scalar, "s", "activation", out=rstd2b[g % 3],
                                in_=mstA, func=AF.Exp, scale=-0.5,
                                inc="rstd2S")

                @blk.gpsimd
                def _(gpsimd):
                    for g in range(NG):
                        self.W(gpsimd, "gnr", 16 * (g + 1))
                        dst = _ap(self.g_d, g * 128,
                                  [[L, 128], [128 * L, 16], [1, 128]])
                        self.inc(gpsimd.dma_start(out=dst, in_=gbuf[g % 6]),
                                 f"stg{g % 6}", 16)
                    for j in range(6):
                        if self.cnt.get(f"stg{j}", 0):
                            self.W(gpsimd, f"stg{j}", self.cnt[f"stg{j}"])
        self.end_stage()

    def s3gb(self):
        nc, L = self.nc, self.L
        NT = L // 512
        with ExitStack() as st:
            def sb(name, shape, dt=F32):
                return st.enter_context(nc.sbuf_tensor(name, shape, dt)).ap()
            wo = sb("wo", [128, 16384])
            gin = [sb(f"gin{j}", [128, 8192]) for j in range(2)]
            xg = [sb(f"xg{j}", [128, 4096]) for j in range(2)]
            x2b = sb("x2b0", [128, 4096])
            x2sqb = [sb(f"x2sqb{j}", [128, 512]) for j in range(2)]
            mu2 = sb("mu2", [128, 512])
            tv2 = sb("tv2", [128, 512])
            v2 = sb("v2", [128, 512])
            rstd3 = sb("rstd3", [128, 512])
            mur3a = sb("mur3a", [128, 512])
            mstB = sb("mstB", [128, 512])
            l2h = [sb(f"l2h{j}", [128, 512], F32R) for j in range(2)]
            l2l = [sb(f"l2l{j}", [128, 512], F32R) for j in range(2)]
            lnC = sb("lnC", [128, 512])
            lnSp = sb("lnSp", [128, 512])
            pW = [st.enter_context(nc.psum_tensor(f"pW3{j}", [128, 512], F32)).ap()
                  for j in range(2)]
            pSa = st.enter_context(nc.psum_tensor("pSa2", [128, 512], F32)).ap()
            pSb = st.enter_context(nc.psum_tensor("pSb2", [128, 512], F32)).ap()

            with nc.Block("s3gb") as blk:
                @blk.sync
                def _(sync):
                    self.inc(sync.dma_start(out=wo, in_=self.wOs.ap()), "ldwo", 16)
                    for t in range(NT):
                        if t >= 2:
                            self.W(sync, "teW", 8 * (t - 1))
                        self.inc(sync.dma_start(
                            out=gin[t % 2],
                            in_=_ap(self.g_d, t * 512,
                                    [[L, 128], [128 * L, 16], [1, 512]])),
                            f"ldg{t % 2}", 16)
                        if t >= 2:
                            self.W(sync, "x2a", 8 * (t - 1))
                        self.inc(sync.dma_start(
                            out=xg[t % 2],
                            in_=_ap(self.xT, t * 512,
                                    [[L, 128], [128 * L, 8], [1, 512]])),
                            f"ldx5{t % 2}", 16)

                @blk.tensor
                def _(tensor):
                    self.W(tensor, "ldwo", 16)
                    for t in range(NT):
                        self.W(tensor, f"ldg{t % 2}", 16 * (t // 2 + 1))
                        for m in range(8):
                            idx = t * 8 + m
                            if idx >= 2:
                                self.W(tensor, "x2a", idx - 1)
                            for k in range(16):
                                it = tensor.matmul(
                                    pW[m % 2][:, 0:512],
                                    wo[:, (m * 16 + k) * 128:(m * 16 + k + 1) * 128],
                                    gin[t % 2][:, k * 512:(k + 1) * 512],
                                    start=(k == 0), stop=(k == 15))
                            self.inc(it, "teW", 1)
                        for m in range(8):
                            self.W(tensor, "x2a", t * 8 + m + 1)
                            if m == 0 and t >= 1:
                                self.W(tensor, "v2r", t)
                            it = tensor.matmul(pSa[:, 0:512], self.ones_s.ap(),
                                               x2b[:, m * 512:(m + 1) * 512],
                                               start=(m == 0), stop=(m == 7))
                            self.inc(it, "teSaM", 1)
                        for m in range(8):
                            self.W(tensor, "sqx2", t * 8 + m + 1)
                            it = tensor.matmul(pSb[:, 0:512], self.ones_s.ap(),
                                               x2sqb[m % 2], start=(m == 0),
                                               stop=(m == 7))
                            self.inc(it, "teSbM", 1)

                @blk.vector
                def _(vector):
                    for t in range(NT):
                        self.W(vector, f"ldx5{t % 2}", 16 * (t // 2 + 1))
                        if t >= 1:
                            self.W(vector, "stx2c", 16 * t)
                            self.W(vector, "teSaM", 8 * t)
                            self.W(vector, "sqx2", 8 * t)
                        for m in range(8):
                            self.W(vector, "teW", t * 8 + m + 1)
                            self.EV(vector, "v", "tensor_add",
                                    x2b[:, m * 512:(m + 1) * 512],
                                    xg[t % 2][:, m * 512:(m + 1) * 512],
                                    pW[m % 2][:, 0:512], inc="x2a")
                        self.W(vector, "teSaM", 8 * (t + 1))
                        self.EV(vector, "v", "tensor_scalar_mul", mu2,
                                pSa[:, 0:512], 1.0 / D)
                        self.EV(vector, "v", "tensor_mul", tv2, mu2, mu2)
                        self.W(vector, "teSbM", 8 * (t + 1))
                        self.EV(vector, "v", "scalar_tensor_tensor", out=v2,
                                in0=pSb[:, 0:512], scalar=1.0 / D, in1=tv2,
                                op0=AL.mult, op1=AL.subtract, inc="v2r")
                        self.W(vector, "rstd3S", t + 1)
                        self.EV(vector, "v", "tensor_mul", mur3a, mu2, rstd3)
                        for m in range(8):
                            idx = t * 8 + m
                            if idx >= 2:
                                self.W(vector, f"stl2{idx % 2}",
                                       32 * (idx // 2))
                            self.EV(vector, "v", "tensor_mul", lnC,
                                    x2b[:, m * 512:(m + 1) * 512], rstd3)
                            self.EV(vector, "v", "tensor_sub", lnC,
                                    lnC, mur3a)
                            self.EV(vector, "v", "scalar_tensor_tensor",
                                    out=lnSp, in0=lnC,
                                    scalar=4097.0, in1=lnC,
                                    op0=AL.mult, op1=AL.subtract)
                            self.EV(vector, "v", "scalar_tensor_tensor",
                                    out=l2h[idx % 2], in0=lnC,
                                    scalar=4097.0, in1=lnSp,
                                    op0=AL.mult, op1=AL.subtract)
                            self.EV(vector, "v", "tensor_sub", l2l[idx % 2],
                                    lnC, l2h[idx % 2], inc="ln2s")

                @blk.scalar
                def _(scalar):
                    for t in range(NT):
                        for m in range(8):
                            idx = t * 8 + m
                            self.W(scalar, "x2a", idx + 1)
                            if idx >= 2:
                                self.W(scalar, "teSbM", idx - 1)
                            self.EV(scalar, "s", "activation",
                                    out=x2sqb[m % 2],
                                    in_=x2b[:, m * 512:(m + 1) * 512],
                                    func=AF.Square, inc="sqx2")
                        self.W(scalar, "v2r", t + 1)
                        if t >= 1:
                            self.W(scalar, "ln2s", 8 * t)
                        self.EV(scalar, "s", "activation", out=mstB, in_=v2,
                                func=AF.Ln, bias=self.c_epsln.ap())
                        self.EV(scalar, "s", "activation", out=rstd3,
                                in_=mstB, func=AF.Exp, scale=-0.5,
                                inc="rstd3S")

                @blk.gpsimd
                def _(gpsimd):
                    for t in range(NT):
                        for m in range(8):
                            idx = t * 8 + m
                            self.W(gpsimd, "ln2s", idx + 1)
                            dst = _ap(self.ln2h_d, m * 128 * L + t * 512,
                                      [[L, 128], [1, 512]]).bitcast(F32R)
                            self.inc(gpsimd.dma_start(out=dst, in_=l2h[idx % 2]),
                                     f"stl2{idx % 2}", 16)
                            dst = _ap(self.ln2l_d, m * 128 * L + t * 512,
                                      [[L, 128], [1, 512]]).bitcast(F32R)
                            self.inc(gpsimd.dma_start(out=dst, in_=l2l[idx % 2]),
                                     f"stl2{idx % 2}", 16)
                        self.W(gpsimd, "x2a", 8 * (t + 1))
                        dst = _ap(self.x2_d, t * 512,
                                  [[L, 128], [128 * L, 8], [1, 512]])
                        self.inc(gpsimd.dma_start(out=dst, in_=x2b), "stx2c", 16)
                    for nm in ("stl20", "stl21", "stx2c"):
                        if self.cnt.get(nm, 0):
                            self.W(gpsimd, nm, self.cnt[nm])
        self.end_stage()

    def s4(self):
        nc, L, NTT = self.nc, self.L, self.NTT
        with ExitStack() as st:
            def sb(name, shape, dt=F32):
                return st.enter_context(nc.sbuf_tensor(name, shape, dt)).ap()
            w1h = sb("w1h4", [128, 16384], F32R)
            w1l = sb("w1l4", [128, 16384], F32R)
            linh = [sb(f"linh4{j}", [128, 4096], F32R) for j in range(2)]
            linl = [sb(f"linl4{j}", [128, 4096], F32R) for j in range(2)]
            hgb = [sb(f"hgb{j}", [128, TT]) for j in range(2)]
            hgh = [sb(f"hgh{j}", [128, TT], F32R) for j in range(2)]
            hgl = [sb(f"hgl{j}", [128, TT], F32R) for j in range(2)]
            pF = [st.enter_context(nc.psum_tensor(f"pF4{j}", [128, 512], F32)).ap()
                  for j in range(2)]
            pT4 = st.enter_context(nc.psum_tensor("pT4", [128, 512], F32)).ap()
            NU = NTT * 32

            with nc.Block("s4") as blk:
                @blk.sync
                def _(sync):
                    def wload(half):
                        for q in range(4):
                            if half == 1:
                                self.W(sync, "teF4",
                                       (NTT - 1) * 16 + 4 * q + 4)
                            dst = w1l[:, q * 4096:(q + 1) * 4096]
                            srcw = self.w1s.ap()[:, half * 16384 + q * 4096:
                                                 half * 16384 + (q + 1) * 4096]
                            srcw = srcw.bitcast(F32R)
                            self.inc(sync.dma_start(out=dst, in_=srcw),
                                     f"ldw4q{q}", 16)
                    def linload(t):
                        half, i = t // NTT, t % NTT
                        if t >= 2:
                            self.W(sync, "teF4", (t - 1) * 16)
                        srcl = _ap(self.ln2h_d, i * TT,
                                   [[L, 128], [128 * L, 8], [1, TT]]).bitcast(F32R)
                        self.inc(sync.dma_start(out=linh[t % 2], in_=srcl),
                                 f"ldh4{t % 2}", 16)
                        srcl = _ap(self.ln2l_d, i * TT,
                                   [[L, 128], [128 * L, 8], [1, TT]]).bitcast(F32R)
                        self.inc(sync.dma_start(out=linl[t % 2], in_=srcl),
                                 f"ldl4{t % 2}", 16)
                    wload(0)
                    for t in range(min(NTT + 2, 2 * NTT)):
                        linload(t)
                    wload(1)
                    for t in range(min(NTT + 2, 2 * NTT), 2 * NTT):
                        linload(t)

                @blk.vector
                def _(vector):
                    def wsplit(half):
                        for c in range(32):
                            self.W(vector, f"ldw4q{c // 8}", 16 * (half + 1))
                            if half == 1:
                                self.W(vector, "teF4",
                                       (NTT - 1) * 16 + c // 2 + 1)
                            xs = w1l[:, c * 512:(c + 1) * 512]
                            hs = w1h[:, c * 512:(c + 1) * 512]
                            self.EV(vector, "v", "scalar_tensor_tensor",
                                    out=pT4[:, 0:512], in0=xs, scalar=4097.0,
                                    in1=xs, op0=AL.mult, op1=AL.subtract)
                            self.EV(vector, "v", "scalar_tensor_tensor",
                                    out=hs, in0=xs, scalar=4097.0,
                                    in1=pT4[:, 0:512], op0=AL.mult,
                                    op1=AL.subtract)
                            self.EV(vector, "v", "tensor_sub", xs, xs, hs,
                                    inc="wsp4")
                    for half in range(2):
                        wsplit(half)
                        for i in range(NTT):
                            for ml in range(16):
                                idx = (half * NTT + i) * 16 + ml
                                self.W(vector, "evF4", idx + 1)
                                if idx >= 2:
                                    self.W(vector, f"sth{idx % 2}",
                                           32 * (idx // 2))
                                h = hgb[idx % 2]
                                self.EV(vector, "v", "scalar_tensor_tensor",
                                        out=pT4[:, 0:TT], in0=h, scalar=4097.0,
                                        in1=h, op0=AL.mult, op1=AL.subtract)
                                self.EV(vector, "v", "scalar_tensor_tensor",
                                        out=hgh[idx % 2], in0=h, scalar=4097.0,
                                        in1=pT4[:, 0:TT], op0=AL.mult,
                                        op1=AL.subtract)
                                self.EV(vector, "v", "tensor_sub",
                                        hgl[idx % 2], h, hgh[idx % 2],
                                        inc="spl4")

                @blk.tensor
                def _(tensor):
                    for half in range(2):
                        for i in range(NTT):
                            t = half * NTT + i
                            self.W(tensor, f"ldh4{t % 2}", 16 * (t // 2 + 1))
                            self.W(tensor, f"ldl4{t % 2}", 16 * (t // 2 + 1))
                            for ml in range(16):
                                idx = t * 16 + ml
                                if i == 0:
                                    self.W(tensor, "wsp4",
                                           half * 32 + 2 * (ml + 1))
                                if idx >= 2:
                                    self.W(tensor, "evF4", idx - 1)
                                for k in range(8):
                                    co = (ml * 8 + k) * 128
                                    it = tensor.matmul(
                                        pF[idx % 2][:, 0:TT],
                                        w1h[:, co:co + 128],
                                        linh[t % 2][:, k * TT:(k + 1) * TT],
                                        start=(k == 0), stop=False)
                                    it = tensor.matmul(
                                        pF[idx % 2][:, 0:TT],
                                        w1h[:, co:co + 128],
                                        linl[t % 2][:, k * TT:(k + 1) * TT],
                                        start=False, stop=False)
                                    it = tensor.matmul(
                                        pF[idx % 2][:, 0:TT],
                                        w1l[:, co:co + 128],
                                        linh[t % 2][:, k * TT:(k + 1) * TT],
                                        start=False, stop=(k == 7))
                                self.inc(it, "teF4", 1)

                @blk.scalar
                def _(scalar):
                    for half in range(2):
                        for i in range(NTT):
                            for ml in range(16):
                                idx = (half * NTT + i) * 16 + ml
                                m = half * 16 + ml
                                self.W(scalar, "teF4", idx + 1)
                                if idx >= 2:
                                    self.W(scalar, "spl4", idx - 1)
                                self.EV(scalar, "s", "activation",
                                        out=hgb[idx % 2],
                                        in_=pF[idx % 2][:, 0:TT], func=AF.Gelu,
                                        bias=self.b1_s.ap()[:, m:m + 1],
                                        inc="evF4")

                @blk.gpsimd
                def _(gpsimd):
                    for half in range(2):
                        for i in range(NTT):
                            for ml in range(16):
                                idx = (half * NTT + i) * 16 + ml
                                m = half * 16 + ml
                                self.W(gpsimd, "spl4", idx + 1)
                                dst = _ap(self.hgh_d, m * 128 * L + i * TT,
                                          [[L, 128], [1, TT]]).bitcast(F32R)
                                self.inc(gpsimd.dma_start(out=dst,
                                                          in_=hgh[idx % 2]),
                                         f"sth{idx % 2}", 16)
                                dst = _ap(self.hgl_d, m * 128 * L + i * TT,
                                          [[L, 128], [1, TT]]).bitcast(F32R)
                                self.inc(gpsimd.dma_start(out=dst,
                                                          in_=hgl[idx % 2]),
                                         f"sth{idx % 2}", 16)
                    for j in range(2):
                        if self.cnt.get(f"sth{j}", 0):
                            self.W(gpsimd, f"sth{j}", self.cnt[f"sth{j}"])
        self.end_stage()

    def s5(self):
        nc, L, NTT = self.nc, self.L, self.NTT
        with ExitStack() as st:
            def sb(name, shape, dt=F32):
                return st.enter_context(nc.sbuf_tensor(name, shape, dt)).ap()
            w2 = sb("w2b", [128, 32768])
            w2e = sb("w2e", [128, 1024])
            w2h = [sb(f"w2h{j}", [128, 1024], F32R) for j in range(2)]
            w2l = [sb(f"w2l{j}", [128, 1024], F32R) for j in range(2)]
            hgh = [sb(f"hgh5{j}", [128, TT], F32R) for j in range(4)]
            hgl = [sb(f"hgl5{j}", [128, TT], F32R) for j in range(4)]
            x2t = [sb("x2t0", [128, 4096])]
            obuf = sb("obuf", [128, 4096])
            pO = [st.enter_context(nc.psum_tensor(f"pO5{j}", [128, 512], F32)).ap()
                  for j in range(8)]

            with nc.Block("s5") as blk:
                @blk.sync
                def _(sync):
                    self.inc(sync.dma_start(out=w2, in_=self.w2s.ap()), "ldw5", 16)
                    for i in range(NTT):
                        if i >= 1:
                            self.W(sync, "evO5", 8 * i)
                        src = _ap(self.x2_d, i * TT, [[L, 128], [128 * L, 8], [1, TT]])
                        self.inc(sync.dma_start(out=x2t[0], in_=src),
                                 "ldx50", 16)
                        for k in range(32):
                            gk = i * 32 + k
                            if gk >= 4:
                                self.W(sync, "teO5", (gk - 3) * 24)
                            src = _ap(self.hgh_d, k * 128 * L + i * TT,
                                      [[L, 128], [1, TT]]).bitcast(F32R)
                            self.inc(sync.dma_start(out=hgh[k % 4], in_=src),
                                     f"ldhh{k % 4}", 16)

                @blk.scalar
                def _(scalar):
                    for i in range(NTT):
                        for k in range(32):
                            gk = i * 32 + k
                            if gk >= 4:
                                self.W(scalar, "teO5", (gk - 3) * 24)
                            src = _ap(self.hgl_d, k * 128 * L + i * TT,
                                      [[L, 128], [1, TT]]).bitcast(F32R)
                            self.inc(scalar.dma_start(out=hgl[k % 4], in_=src),
                                     f"qshl{k % 4}", 16)

                @blk.vector
                def _(vector):
                    self.W(vector, "ldw5", 16)
                    for i in range(NTT):
                        for k in range(32):
                            gk = i * 32 + k
                            if gk >= 2:
                                self.W(vector, "teO5", (gk - 1) * 24)
                            wsl = w2[:, k * 1024:(k + 1) * 1024]
                            self.EV(vector, "v", "scalar_tensor_tensor",
                                    out=w2e, in0=wsl, scalar=4097.0, in1=wsl,
                                    op0=AL.mult, op1=AL.subtract)
                            self.EV(vector, "v", "scalar_tensor_tensor",
                                    out=w2h[k % 2], in0=wsl, scalar=4097.0, in1=w2e,
                                    op0=AL.mult, op1=AL.subtract)
                            self.EV(vector, "v", "tensor_sub",
                                    w2l[k % 2], wsl, w2h[k % 2], inc="spwv")
                        self.W(vector, "ldx50", 16 * (i + 1))
                        self.W(vector, "teO5", 768 * (i + 1))
                        if i >= 1:
                            self.W(vector, "stout", 16 * i)
                        for m in range(8):
                            self.EV(vector, "v", "scalar_tensor_tensor",
                                    out=obuf[:, m * TT:(m + 1) * TT],
                                    in0=x2t[0][:, m * TT:(m + 1) * TT],
                                    scalar=self.b2_s.ap()[:, m:m + 1],
                                    in1=pO[m][:, 0:TT],
                                    op0=AL.add, op1=AL.add, inc="evO5")

                @blk.gpsimd
                def _(gpsimd):
                    for i in range(NTT):
                        if i >= 1:
                            self.W(gpsimd, "evO5", 8 * i)
                            dst = _ap(self.outT, (i - 1) * TT,
                                      [[L, 128], [128 * L, 8], [1, TT]])
                            self.inc(gpsimd.dma_start(out=dst, in_=obuf), "stout", 16)
                    self.W(gpsimd, "evO5", 8 * NTT)
                    dst = _ap(self.outT, (NTT - 1) * TT,
                              [[L, 128], [128 * L, 8], [1, TT]])
                    self.inc(gpsimd.dma_start(out=dst, in_=obuf), "stout", 16)
                    self.W(gpsimd, "stout", self.cnt["stout"])

                @blk.tensor
                def _(tensor):
                    for i in range(NTT):
                        if i >= 1:
                            self.W(tensor, "evO5", 8 * i)
                        for k in range(32):
                            gk = i * 32 + k
                            self.W(tensor, f"ldhh{k % 4}", 16 * (gk // 4 + 1))
                            self.W(tensor, f"qshl{k % 4}", 16 * (gk // 4 + 1))
                            self.W(tensor, "spwv", gk + 1)
                            wh, wl = w2h[k % 2], w2l[k % 2]
                            for m in range(8):
                                it = tensor.matmul(
                                    pO[m][:, 0:TT],
                                    wh[:, m * 128:(m + 1) * 128],
                                    hgh[k % 4], start=(k == 0), stop=False)
                                self.inc(it, "teO5", 1)
                                it = tensor.matmul(
                                    pO[m][:, 0:TT],
                                    wh[:, m * 128:(m + 1) * 128],
                                    hgl[k % 4], start=False, stop=False)
                                self.inc(it, "teO5", 1)
                                it = tensor.matmul(
                                    pO[m][:, 0:TT],
                                    wl[:, m * 128:(m + 1) * 128],
                                    hgh[k % 4], start=False,
                                    stop=(k == 31 and True))
                                self.inc(it, "teO5", 1)

    def build(self):
        for s in ("s0", "s1a", "s1b", "s2", "s3z", "s3r", "s3ga", "s3gb", "s4",
                  "s5"):
            if s in self.stages:
                getattr(self, s)()
        return self.nc


def kernel(**inputs):
    x = np.asarray(inputs["L0_attn_x"])
    L = x.shape[1]
    common = host_prep(inputs)
    mk = MK(L=L)
    nc = mk.build()
    in_maps = []
    for b in range(8):
        m = dict(common)
        m["xT"] = np.ascontiguousarray(x[b].T)
        in_maps.append(m)
    res = run_bass_kernel_spmd(nc, in_maps, core_ids=list(range(8)))
    out = np.stack([np.asarray(r["outT"]).T for r in res.results], axis=0)
    return np.ascontiguousarray(out.astype(np.float32))
